# version 10
# baseline (speedup 1.0000x reference)
import contextlib
import os

import numpy as np
import ml_dtypes

import concourse.bass as bass
import concourse.mybir as mybir
from concourse.bass_utils import run_bass_kernel_spmd

F32 = mybir.dt.float32
BF16 = mybir.dt.bfloat16
U8 = mybir.dt.uint8
AX = mybir.AxisListType.X
ALU = mybir.AluOpType
AF = mybir.ActivationFunctionType

B, T, D = 4, 4096, 1024
H, HD = 16, 64
NB = 32
NJ = 16
TO = NJ * 128
LN_EPS = 1e-5
VROW = H * 65

bf16 = ml_dtypes.bfloat16



class _W:

    def __init__(self, e):
        self._e = e
        self._last = None

    def __getattr__(self, name):
        f = getattr(self._e, name)
        if callable(f):
            def g(*a, **kw):
                r = f(*a, **kw)
                if hasattr(r, "then_inc") and name not in ("wait_ge", "sem_inc"):
                    self._last = r
                return r
            return g
        return f

    def mark(self, sem, n=1):
        assert self._last is not None, "mark() with no preceding instruction"
        self._last.then_inc(sem, n)
        self._last = None


def own_blocks(role):
    out = []
    for j in range(NJ):
        if role == 0:
            out.append(2 * j + (j & 1))
        else:
            out.append(2 * j + 1 - (j & 1))
    return out


def nq_of(kb):
    return 128 * (NJ - kb // 2)


def attn_steps():
    steps = []
    for kb in range(NB):
        qs = TO - nq_of(kb)
        jm = kb // 2
        for c in (0, 1):
            r0 = max(qs, c * 1024)
            r1 = (c + 1) * 1024
            if r0 >= r1:
                continue
            mask = (jm, kb - 2 * jm) if (jm * 128 >= r0 and jm * 128 < r1) else None
            steps.append((kb, c, r0, r1, mask))
    return steps


STEPS = attn_steps()
NST = len(STEPS)
MASKED = [i for i, s in enumerate(STEPS) if s[4] is not None]
MPREF = []
_c = 0
for _i in range(NST):
    if STEPS[_i][4] is not None:
        _c += 1
    MPREF.append(_c)
NMH = MPREF[-1]


def split512(a, b):
    out = []
    while a < b:
        nxt = min(b, (a // 512 + 1) * 512)
        out.append((a, nxt))
        a = nxt
    return out


def build_program(upto=99):
    K_LN = os.environ.get("K_LN", "1") == "1"
    K_APP = os.environ.get("K_APP", "1") == "1"
    K_RSTD = os.environ.get("K_RSTD", "1") == "1"
    K_SQ = os.environ.get("K_SQ", "1") == "1"
    K_RED = os.environ.get("K_RED", "1") == "1"
    K_TR = os.environ.get("K_TR", "1") == "1"
    nc = bass.Bass("TRN2", target_bir_lowering=False, debug=False, detect_race_conditions=False)

    i_xT = nc.dram_tensor("xT", [D, T], F32, kind="ExternalInput").ap()
    i_xq = nc.dram_tensor("xq", [TO, D], F32, kind="ExternalInput").ap()
    i_xqT = nc.dram_tensor("xqT", [D, TO], F32, kind="ExternalInput").ap()
    i_wkT = nc.dram_tensor("wkT", [D, D], F32, kind="ExternalInput").ap()
    i_wvT = nc.dram_tensor("wvT", [D, D], F32, kind="ExternalInput").ap()
    i_wqT = nc.dram_tensor("wqT", [D, D], F32, kind="ExternalInput").ap()
    i_wuT = nc.dram_tensor("wuT", [D, D], F32, kind="ExternalInput").ap()
    i_klnw = nc.dram_tensor("klnw", [1, HD], F32, kind="ExternalInput").ap()
    i_qlnw = nc.dram_tensor("qlnw", [1, HD], F32, kind="ExternalInput").ap()
    i_qlnb = nc.dram_tensor("qlnb", [1, HD], F32, kind="ExternalInput").ap()
    i_k0 = nc.dram_tensor("k0", [2 * D, 4096], U8, kind="ExternalInput").ap()
    i_k1 = nc.dram_tensor("k1", [4096, 4096], U8, kind="ExternalInput").ap()
    i_k2 = nc.dram_tensor("k2", [4096, D], U8, kind="ExternalInput").ap()
    i_s0 = nc.dram_tensor("s0", [1, 1], F32, kind="ExternalInput").ap()
    i_s1 = nc.dram_tensor("s1", [1, 1], F32, kind="ExternalInput").ap()
    i_s2 = nc.dram_tensor("s2", [1, 1], F32, kind="ExternalInput").ap()
    i_ident = nc.dram_tensor("ident", [128, 128], BF16, kind="ExternalInput").ap()
    i_masks = nc.dram_tensor("masks", [NJ, 2, 128, 128], BF16, kind="ExternalInput").ap()
    o_out = nc.dram_tensor("out", [TO, D], F32, kind="ExternalOutput").ap()
    K_DBG = os.environ.get("K_DBG", "0") == "1"
    if K_DBG:
        o_dbgs = nc.dram_tensor("dbgs", [128, 64], F32, kind="ExternalOutput").ap()
        o_dbgy = nc.dram_tensor("dbgy", [NJ, 128, D], F32, kind="ExternalOutput").ap()
        o_dbgk = nc.dram_tensor("dbgk", [8, 128, T], BF16, kind="ExternalOutput").ap()
        o_dbga = nc.dram_tensor("dbga", [4, 128, TO], BF16, kind="ExternalOutput").ap()
    d_kT = nc.dram_tensor("kT_scr", [8, 128, T], BF16, kind="Internal").ap()
    d_y = nc.dram_tensor("y_scr", [NJ, 128, D], F32, kind="Internal").ap()
    d_a2 = nc.dram_tensor("a2_scr", [32, 128, TO], BF16, kind="Internal").ap()
    d_r = nc.dram_tensor("r_scr", [1, TO], F32, kind="Internal").ap()

    es = contextlib.ExitStack()
    with es:
        s_dbgs = es.enter_context(nc.sbuf_tensor("s_dbgs", [128, 64], F32))
        ident = es.enter_context(nc.sbuf_tensor("sb_ident", [128, 128], BF16))
        masks = es.enter_context(nc.sbuf_tensor("sb_masks", [128, NJ, 2, 128], BF16))
        onesf = es.enter_context(nc.sbuf_tensor("onesf", [1, 128], F32))
        eps = es.enter_context(nc.sbuf_tensor("eps", [128, 1], F32))
        tiny = es.enter_context(nc.sbuf_tensor("tiny", [128, 1], F32))
        half = es.enter_context(nc.sbuf_tensor("half", [128, 1], F32))
        mhalf = es.enter_context(nc.sbuf_tensor("mhalf", [128, 1], F32))
        two_b = es.enter_context(nc.sbuf_tensor("two_b", [128, 1], F32))
        mone_b = es.enter_context(nc.sbuf_tensor("mone_b", [128, 1], F32))
        wrep8 = es.enter_context(nc.sbuf_tensor("wrep8", [128, HD], F32))
        brep8 = es.enter_context(nc.sbuf_tensor("brep8", [128, HD], F32))
        sc2s0 = es.enter_context(nc.sbuf_tensor("sc2s0", [128, 1], F32))
        sc2s1 = es.enter_context(nc.sbuf_tensor("sc2s1", [128, 1], F32))
        sc2s2 = es.enter_context(nc.sbuf_tensor("sc2s2", [128, 1], F32))
        row_a = es.enter_context(nc.sbuf_tensor("row_a", [1, HD], F32))
        row_b = es.enter_context(nc.sbuf_tensor("row_b", [1, HD], F32))
        klnw = es.enter_context(nc.sbuf_tensor("klnw_s", [1, HD], F32))
        qlnw = es.enter_context(nc.sbuf_tensor("qlnw_s", [1, HD], F32))
        qlnb = es.enter_context(nc.sbuf_tensor("qlnb_s", [1, HD], F32))
        sclr = es.enter_context(nc.sbuf_tensor("sclr", [1, 3], F32))
        mu16 = es.enter_context(nc.sbuf_tensor("mu16", [128, 16], F32))
        m216 = es.enter_context(nc.sbuf_tensor("m216", [128, 16], F32))
        var16 = es.enter_context(nc.sbuf_tensor("var16", [128, 16], F32))
        rk16 = es.enter_context(nc.sbuf_tensor("rk16", [128, 16], F32))
        nrm16 = es.enter_context(nc.sbuf_tensor("nrm16", [128, 16], F32))
        nb128 = es.enter_context(nc.sbuf_tensor("nb128", [128, 16], BF16))
        mu1 = es.enter_context(nc.sbuf_tensor("mu1", [128, 1], F32))
        m21 = es.enter_context(nc.sbuf_tensor("m21", [128, 1], F32))
        var1 = es.enter_context(nc.sbuf_tensor("var1", [128, 1], F32))
        rs1 = es.enter_context(nc.sbuf_tensor("rs1", [128, 1], F32))

        SC = es.enter_context(nc.semaphore("SC"))
        SW = [es.enter_context(nc.semaphore(f"SW{i}")) for i in range(2)]
        SXA = [es.enter_context(nc.semaphore(f"SXA{i}")) for i in range(2)]
        SXB = [es.enter_context(nc.semaphore(f"SXB{i}")) for i in range(2)]
        SXC = [es.enter_context(nc.semaphore(f"SXC{i}")) for i in range(2)]
        SGA = [es.enter_context(nc.semaphore(f"SGA{i}")) for i in range(2)]
        SGB = [es.enter_context(nc.semaphore(f"SGB{i}")) for i in range(2)]
        SGC = [es.enter_context(nc.semaphore(f"SGC{i}")) for i in range(2)]
        ST = es.enter_context(nc.semaphore("ST"))
        SAV = es.enter_context(nc.semaphore("SAV"))
        SRR = es.enter_context(nc.semaphore("SRR"))
        SRB = [es.enter_context(nc.semaphore(f"SRB{i}")) for i in range(2)]
        SM = es.enter_context(nc.semaphore("SM"))
        SV = es.enter_context(nc.semaphore("SV"))
        SS = es.enter_context(nc.semaphore("SS"))
        SO = [es.enter_context(nc.semaphore(f"SO{i}")) for i in range(2)]
        BAR = es.enter_context(nc.semaphore("BAR"))
        SVF = es.enter_context(nc.semaphore("SVF"))

        C = dict(t=0, av=0, rr=0, v=0, s=0, bar=0)
        _fence_n = [0]

        def vfence(vec):
            if vec._last is None:
                return
            _fence_n[0] += 1
            vec._last.then_inc(SVF, 1)
            vec._last = None
            vec.wait_ge(SVF, _fence_n[0])
        all_stacks = []

        p0 = contextlib.ExitStack(); all_stacks.append(p0)
        s_wst = [p0.enter_context(nc.sbuf_tensor(f"s_wst{i}", [128, 1024], F32, side="right")) for i in range(2)]
        s_wkT = p0.enter_context(nc.sbuf_tensor("s_wkT", [128, 8, 1024], BF16, side="right"))
        s_wvT = p0.enter_context(nc.sbuf_tensor("s_wvT", [128, 8, 1024], BF16, side="right"))
        vh_es = contextlib.ExitStack(); all_stacks.append(vh_es)
        vhat = vh_es.enter_context(nc.sbuf_tensor("vhat", [128, NB * VROW], BF16))
        pb_es = contextlib.ExitStack(); all_stacks.append(pb_es)
        p_bc = pb_es.enter_context(nc.psum_tensor("p_bc", [128, 512], F32))

        v0, t0 = C["v"], C["t"]
        with nc.Block("p0") as blk:
            @blk.sync
            def _(sync):
                sync = _W(sync)
                sync.dma_start(out=ident[:], in_=i_ident).then_inc(SC, 16)
                sync.dma_start(out=masks[:], in_=i_masks.rearrange("j t k q -> k j t q")).then_inc(SC, 16)
                sync.dma_start(out=klnw[:], in_=i_klnw).then_inc(SC, 16)
                sync.dma_start(out=qlnw[:], in_=i_qlnw).then_inc(SC, 16)
                sync.dma_start(out=qlnb[:], in_=i_qlnb).then_inc(SC, 16)
                sync.dma_start(out=sclr[:, 0:1], in_=i_s0).then_inc(SC, 16)
                sync.dma_start(out=sclr[:, 1:2], in_=i_s1).then_inc(SC, 16)
                sync.dma_start(out=sclr[:, 2:3], in_=i_s2).then_inc(SC, 16)
                for k in range(16):
                    wsrc = i_wkT if k < 8 else i_wvT
                    ec = k % 8
                    if k >= 2:
                        sync.wait_ge(SW[k % 2], 16 * (k // 2))
                        sync.wait_ge(SV, v0 + k)
                    sync.dma_start(out=s_wst[k % 2][:], in_=wsrc[ec * 128:(ec + 1) * 128, :]).then_inc(SW[k % 2], 16)
                sync.wait_ge(SW[0], 16 * 8)
                sync.wait_ge(SW[1], 16 * 8)
                sync.sem_inc(BAR, 1)

            @blk.vector
            def _(vector):
                vector = _W(vector)
                vector.wait_ge(SC, 16 * 8)
                vector.memset(onesf[:], 1.0)
                vector.memset(eps[:], LN_EPS)
                vector.memset(tiny[:], 1e-35)
                vector.memset(half[:], 0.5)
                vector.memset(mhalf[:], -0.5)
                vector.memset(two_b[:], 2.0)
                vector.memset(mone_b[:], -1.0)
                vector.memset(vhat[:], 1.0)
                vector.tensor_tensor(row_a[:], qlnw[:], klnw[:], ALU.mult)
                vector.tensor_tensor(row_b[:], qlnb[:], klnw[:], ALU.mult)
                vfence(vector)
                vector.tensor_scalar_mul(row_a[:], row_a[:], 0.125)
                vector.tensor_scalar_mul(row_b[:], row_b[:], 0.125)
                vector.mark(SV, 1)
                for k in range(16):
                    vector.wait_ge(SW[k % 2], 16 * (k // 2 + 1))
                    dst = s_wkT if k < 8 else s_wvT
                    vector.tensor_copy(dst[:, k % 8, :], s_wst[k % 2][:])
                    vector.mark(SV, 1)
                vector.wait_ge(ST, t0 + 1)
                vector.tensor_copy(wrep8[:], p_bc[:, 0:HD])
                vector.tensor_copy(brep8[:], p_bc[:, 64:64 + HD])
                vector.tensor_scalar(sc2s0[:], p_bc[:, 128:129], 2.0, None, ALU.mult)
                vector.tensor_scalar(sc2s1[:], p_bc[:, 129:130], 2.0, None, ALU.mult)
                vector.tensor_scalar(sc2s2[:], p_bc[:, 130:131], 2.0, None, ALU.mult)
                vector.mark(SV, 1)
                vector.wait_ge(SV, v0 + 18)
                vector.sem_inc(BAR, 1)

            @blk.tensor
            def _(tensor):
                tensor = _W(tensor)
                tensor.wait_ge(SV, v0 + 1)
                tensor.matmul(p_bc[:, 0:HD], onesf[:], row_a[:], start=True, stop=False)
                tensor.matmul(p_bc[:, 64:64 + HD], onesf[:], row_b[:], start=False, stop=False)
                tensor.matmul(p_bc[:, 128:129], onesf[:], sclr[:, 0:1], start=False, stop=False)
                tensor.matmul(p_bc[:, 129:130], onesf[:], sclr[:, 1:2], start=False, stop=False)
                tensor.matmul(p_bc[:, 130:131], onesf[:], sclr[:, 2:3], start=False, stop=True).then_inc(ST, 1)

            @blk.scalar
            def _(scalar):
                scalar = _W(scalar)
                scalar.sem_inc(BAR, 1)

            @blk.gpsimd
            def _(gpsimd):
                gpsimd = _W(gpsimd)
                gpsimd.sem_inc(BAR, 1)

        C["v"] += 18
        C["t"] += 1
        C["bar"] += 4

        if upto <= 0:
            stub = contextlib.ExitStack()
            s_z = stub.enter_context(nc.sbuf_tensor("s_zstub", [128, 1024], F32, side="right"))
            with nc.Block("stub") as blk:
                @blk.gpsimd
                def _(gpsimd):
                    gpsimd = _W(gpsimd)
                    gpsimd.wait_ge(BAR, C["bar"])
                    gpsimd.memset(s_z[:], 0.0)
                    for j in range(NJ):
                        if j >= 2:
                            gpsimd.wait_ge(SO[j % 2], 16 * (j // 2))
                        gpsimd.dma_start(out=o_out[j * 128:(j + 1) * 128, :], in_=s_z[:]).then_inc(SO[j % 2], 16)
                    gpsimd.wait_ge(SO[0], 16 * 8)
                    gpsimd.wait_ge(SO[1], 16 * 8)
                @blk.sync
                def _(sync):
                    sync = _W(sync)
                    sync.sem_inc(BAR, 1)
                @blk.vector
                def _(vector):
                    vector = _W(vector)
                    vector.sem_inc(BAR, 1)
                @blk.scalar
                def _(scalar):
                    scalar = _W(scalar)
                    scalar.sem_inc(BAR, 1)
            stub.close()
            for _st in reversed(all_stacks):
                _st.close()
            return nc

        pb_es.close()

        p1 = contextlib.ExitStack(); all_stacks.append(p1)
        scr32 = p1.enter_context(nc.sbuf_tensor("scr32_a", [128, 1024], F32, side="right"))
        s_xt = [p1.enter_context(nc.sbuf_tensor(f"s_xt{i}", [128, 8, 128], F32, side="right")) for i in range(2)]
        s_xtb = [p1.enter_context(nc.sbuf_tensor(f"s_xtb{i}", [128, 8, 128], BF16, side="right")) for i in range(2)]
        s_khat = [p1.enter_context(nc.sbuf_tensor(f"s_khat{i}", [128, 1024], BF16, side="right")) for i in range(2)]
        s_ktd = [p1.enter_context(nc.sbuf_tensor(f"s_ktd{i}", [128, 1024], BF16, side="right")) for i in range(2)]
        pp1 = contextlib.ExitStack(); all_stacks.append(pp1)
        p_k = [pp1.enter_context(nc.psum_tensor(f"p_k{i}", [128, 1024], F32)) for i in range(2)]
        p_v = [pp1.enter_context(nc.psum_tensor(f"p_v{i}", [128, 1024], F32)) for i in range(2)]

        t1, v1, s1_, b1 = C["t"], C["v"], C["s"], C["bar"]
        V1cast = lambda tb: v1 + tb + 1 if tb <= 1 else v1 + 3 * tb - 1
        V1var = lambda tb: v1 + 3 + 3 * tb
        V1app = lambda tb: v1 + 4 + 3 * tb
        T1kv = lambda tb: t1 + (1 if tb == 0 else 2 * tb)
        T1tr = lambda tb: t1 + (2 * tb + 3 if tb <= NB - 2 else 2 * NB)
        S1 = lambda tb, k: s1_ + 4 * tb + k

        with nc.Block("p1") as blk:
            @blk.sync
            def _(sync):
                sync = _W(sync)
                sync.wait_ge(BAR, b1)
                for tb in range(NB):
                    if tb >= 2:
                        sync.wait_ge(SXA[tb % 2], 16 * (tb // 2))
                        sync.wait_ge(SV, V1cast(tb - 2))
                    sync.dma_start(
                        out=s_xt[tb % 2][:],
                        in_=i_xT[:, tb * 128:(tb + 1) * 128].rearrange("(c p) t -> p c t", p=128),
                    ).then_inc(SXA[tb % 2], 16)
                sync.wait_ge(SXA[0], 16 * 16)
                sync.wait_ge(SXA[1], 16 * 16)
                sync.sem_inc(BAR, 1)

            @blk.tensor
            def _(tensor):
                tensor = _W(tensor)
                tensor.wait_ge(BAR, b1)

                def emit_kv(tb):
                    pp = tb % 2
                    tensor.wait_ge(SV, V1cast(tb))
                    if tb >= 2:
                        tensor.wait_ge(SV, V1app(tb - 2))
                        tensor.wait_ge(SS, S1(tb - 2, 4))
                    mm = None
                    for ec in range(8):
                        st, sp = (ec == 0), (ec == 7)
                        xb = s_xtb[pp][:, ec, :]
                        tensor.matmul(p_k[pp][:, 0:512], xb, s_wkT[:, ec, 0:512], start=st, stop=sp)
                        tensor.matmul(p_k[pp][:, 512:1024], xb, s_wkT[:, ec, 512:1024], start=st, stop=sp)
                        tensor.matmul(p_v[pp][:, 0:512], xb, s_wvT[:, ec, 0:512], start=st, stop=sp)
                        mm = tensor.matmul(p_v[pp][:, 512:1024], xb, s_wvT[:, ec, 512:1024], start=st, stop=sp)
                    mm.then_inc(ST, 1)

                def emit_tr(tb):
                    pp = tb % 2
                    tensor.wait_ge(SV, V1app(tb))
                    tensor.wait_ge(SS, S1(tb, 2))
                    mm = None
                    for c in range(8):
                        mm = tensor.matmul(p_v[pp][:, c * 128:(c + 1) * 128],
                                           s_khat[pp][:, c * 128:(c + 1) * 128], ident[:],
                                           start=(c % 4 == 0), stop=(c % 4 == 3))
                    mm.then_inc(ST, 1)

                emit_kv(0)
                for tb in range(1, NB):
                    emit_kv(tb)
                    emit_tr(tb - 1)
                emit_tr(NB - 1)

            @blk.scalar
            def _(scalar):
                scalar = _W(scalar)
                scalar.wait_ge(BAR, b1)
                for tb in range(NB):
                    pp = tb % 2
                    scalar.wait_ge(ST, T1kv(tb))
                    if tb >= 1:
                        scalar.wait_ge(SV, V1app(tb - 1))
                    scalar.activation(scr32[:], p_k[pp][:], AF.Square)
                    scalar.mark(SS, 1)
                    vh = vhat[:, tb * VROW:(tb + 1) * VROW].rearrange("p (g c) -> p g c", c=65)
                    scalar.copy(vh[:, 0:8, 0:HD], p_v[pp][:, 0:512].rearrange("p (g c) -> p g c", c=HD))
                    scalar.copy(vh[:, 8:16, 0:HD], p_v[pp][:, 512:1024].rearrange("p (g c) -> p g c", c=HD))
                    scalar.mark(SS, 1)
                    scalar.wait_ge(SV, V1var(tb))
                    scalar.activation(rk16[:], var16[:], AF.Ln, bias=eps[:])
                    scalar.activation(rk16[:], rk16[:], AF.Exp, scale=mhalf[:])
                    scalar.mark(SS, 1)
                    scalar.wait_ge(ST, T1tr(tb))
                    if tb >= 2:
                        scalar.wait_ge(SGA[pp], 16 * (tb // 2))
                    scalar.copy(s_ktd[pp][:, 0:512], p_v[pp][:, 0:512])
                    scalar.copy(s_ktd[pp][:, 512:1024], p_v[pp][:, 512:1024])
                    scalar.mark(SS, 1)
                scalar.wait_ge(SS, s1_ + 4 * NB)
                scalar.sem_inc(BAR, 1)

            @blk.vector
            def _(vector):
                vector = _W(vector)
                vector.wait_ge(BAR, b1)

                def emit_cast(tb):
                    pp = tb % 2
                    vector.wait_ge(SXA[pp], 16 * (tb // 2 + 1))
                    if tb >= 2:
                        vector.wait_ge(ST, T1kv(tb - 2))
                    vector.tensor_copy(s_xtb[pp][:], s_xt[pp][:])
                    vector.mark(SV, 1)

                emit_cast(0)
                emit_cast(1)
                for tb in range(NB):
                    pp = tb % 2
                    vector.wait_ge(ST, T1kv(tb))
                    vector.wait_ge(SS, S1(tb, 1))
                    vector.reduce_sum(mu16[:], p_k[pp][:].rearrange("p (g s) -> p g s", s=HD), axis=AX)
                    vector.reduce_sum(m216[:], scr32[:].rearrange("p (g s) -> p g s", s=HD), axis=AX)
                    vfence(vector)
                    vector.tensor_scalar_mul(mu16[:], mu16[:], 1.0 / HD)
                    vector.tensor_scalar_mul(m216[:], m216[:], 1.0 / HD)
                    vfence(vector)
                    vector.tensor_tensor(var16[:], mu16[:], mu16[:], ALU.mult)
                    vfence(vector)
                    vector.tensor_tensor(var16[:], m216[:], var16[:], ALU.subtract)
                    vector.mark(SV, 1)
                    vector.wait_ge(SS, S1(tb, 3))
                    mub = mu16[:].unsqueeze(2).broadcast_to([128, 16, HD])
                    rkb = rk16[:].unsqueeze(2).broadcast_to([128, 16, HD])
                    vector.tensor_tensor(scr32[:].rearrange("p (g s) -> p g s", s=HD),
                                         p_k[pp][:].rearrange("p (g s) -> p g s", s=HD), mub, ALU.subtract)
                    vfence(vector)
                    vector.tensor_tensor(s_khat[pp][:].rearrange("p (g s) -> p g s", s=HD),
                                         scr32[:].rearrange("p (g s) -> p g s", s=HD), rkb, ALU.mult)
                    vector.mark(SV, 1)
                    if tb + 2 < NB:
                        emit_cast(tb + 2)
                    else:
                        vector.memset(tiny[:], 1e-35)
                        vector.mark(SV, 1)
                vector.wait_ge(SV, v1 + 2 + 3 * NB)
                vector.sem_inc(BAR, 1)

            @blk.gpsimd
            def _(gpsimd):
                gpsimd = _W(gpsimd)
                gpsimd.wait_ge(BAR, b1)
                for tb in range(NB):
                    pp = tb % 2
                    gpsimd.wait_ge(SS, S1(tb, 4))
                    if tb >= 2:
                        gpsimd.wait_ge(SGA[pp], 16 * (tb // 2))
                    gpsimd.dma_start(
                        out=d_kT[:, :, tb * 128:(tb + 1) * 128].rearrange("c p t -> p c t"),
                        in_=s_ktd[pp][:].rearrange("p (c t) -> p c t", t=128),
                    ).then_inc(SGA[pp], 16)
                gpsimd.wait_ge(SGA[0], 16 * 16)
                gpsimd.wait_ge(SGA[1], 16 * 16)
                gpsimd.sem_inc(BAR, 1)

                C["t"] = t1 + 2 * NB
        C["v"] = v1 + 2 + 3 * NB
        C["s"] = s1_ + 4 * NB
        C["bar"] = b1 + 4

        if upto <= 1:
            stub = contextlib.ExitStack()
            s_z = stub.enter_context(nc.sbuf_tensor("s_zstub", [128, 1024], F32, side="right"))
            with nc.Block("stub") as blk:
                @blk.gpsimd
                def _(gpsimd):
                    gpsimd = _W(gpsimd)
                    gpsimd.wait_ge(BAR, C["bar"])
                    gpsimd.memset(s_z[:], 0.0)
                    for j in range(NJ):
                        if j >= 2:
                            gpsimd.wait_ge(SO[j % 2], 16 * (j // 2))
                        gpsimd.dma_start(out=o_out[j * 128:(j + 1) * 128, :], in_=s_z[:]).then_inc(SO[j % 2], 16)
                    gpsimd.wait_ge(SO[0], 16 * 8)
                    gpsimd.wait_ge(SO[1], 16 * 8)
                @blk.sync
                def _(sync):
                    sync = _W(sync)
                    sync.sem_inc(BAR, 1)
                @blk.vector
                def _(vector):
                    vector = _W(vector)
                    vector.sem_inc(BAR, 1)
                @blk.scalar
                def _(scalar):
                    scalar = _W(scalar)
                    scalar.sem_inc(BAR, 1)
            stub.close()
            for _st in reversed(all_stacks):
                _st.close()
            return nc

        pp1.close()

        p1b = contextlib.ExitStack(); all_stacks.append(p1b)
        s_wqT = p1b.enter_context(nc.sbuf_tensor("s_wqT", [128, 8, 1024], BF16, side="right"))
        s_t2q = p1b.enter_context(nc.sbuf_tensor("s_t2q", [128, 1024], F32, side="right"))
        s_qtl = [p1b.enter_context(nc.sbuf_tensor(f"s_qtl{i}", [128, 1024], BF16, side="right")) for i in range(2)]
        qT_es = contextlib.ExitStack(); all_stacks.append(qT_es)
        s_qT = qT_es.enter_context(nc.sbuf_tensor("s_qT", [128, 8, 2048], BF16))
        s_negB = qT_es.enter_context(nc.sbuf_tensor("s_negB", [16, 2048], BF16))
        ppb = contextlib.ExitStack(); all_stacks.append(ppb)
        p_q = [ppb.enter_context(nc.psum_tensor(f"p_q{i}", [128, 1024], F32)) for i in range(2)]
        p_qt = ppb.enter_context(nc.psum_tensor("p_qt", [128, 1024], F32))
        p_nb = ppb.enter_context(nc.psum_tensor("p_nb", [16, 128], F32))

        t2, v2, s2_, b2 = C["t"], C["v"], C["s"], C["bar"]
        V2cast = lambda j: v2 + 8 + j + 1 if j <= 1 else v2 + 10 + 6 * (j - 2) + 6
        V2var = lambda j: v2 + 10 + 6 * j + 1
        V2app = lambda j: v2 + 10 + 6 * j + 2
        V2nrm = lambda j: v2 + 10 + 6 * j + 3
        V2nb = lambda j: v2 + 10 + 6 * j + 4
        V2ev = lambda j: v2 + 10 + 6 * j + 5
        T2kv = lambda j: t2 + (1 if j == 0 else 2 * j)
        T2tr = lambda j: t2 + (2 * j + 3 if j <= NJ - 2 else 2 * NJ)
        S2 = lambda j, k: s2_ + 4 * j + k

        with nc.Block("p1b") as blk:
            @blk.sync
            def _(sync):
                sync = _W(sync)
                sync.wait_ge(BAR, b2)
                for k in range(8):
                    sync.wait_ge(SW[k % 2], 16 * (8 + k // 2))
                    if k >= 2:
                        sync.wait_ge(SV, v2 + k - 1)
                    sync.dma_start(out=s_wst[k % 2][:], in_=i_wqT[k * 128:(k + 1) * 128, :]).then_inc(SW[k % 2], 16)
                for j in range(NJ):
                    sync.wait_ge(SXA[j % 2], 16 * (16 + j // 2))
                    if j >= 2:
                        sync.wait_ge(SV, V2cast(j - 2))
                    sync.dma_start(
                        out=s_xt[j % 2][:],
                        in_=i_xqT[:, j * 128:(j + 1) * 128].rearrange("(c p) t -> p c t", p=128),
                    ).then_inc(SXA[j % 2], 16)
                sync.wait_ge(SXA[0], 16 * 24)
                sync.wait_ge(SXA[1], 16 * 24)
                sync.sem_inc(BAR, 1)

            @blk.vector
            def _(vector):
                vector = _W(vector)
                vector.wait_ge(BAR, b2)
                for k in range(8):
                    vector.wait_ge(SW[k % 2], 16 * (8 + k // 2 + 1))
                    vector.tensor_copy(s_wqT[:, k, :], s_wst[k % 2][:])
                    vector.mark(SV, 1)

                def emit_cast(j):
                    pp = j % 2
                    vector.wait_ge(SXA[pp], 16 * (16 + j // 2 + 1))
                    if j >= 2:
                        vector.wait_ge(ST, T2kv(j - 2))
                    vector.tensor_copy(s_xtb[pp][:], s_xt[pp][:])
                    vector.mark(SV, 1)

                emit_cast(0)
                emit_cast(1)
                for j in range(NJ):
                    pp = j % 2
                    vector.wait_ge(ST, T2kv(j))
                    vector.wait_ge(SS, S2(j, 1))
                    vector.reduce_sum(mu16[:], p_q[pp][:].rearrange("p (g s) -> p g s", s=HD), axis=AX)
                    vector.reduce_sum(m216[:], scr32[:].rearrange("p (g s) -> p g s", s=HD), axis=AX)
                    vfence(vector)
                    vector.tensor_scalar_mul(mu16[:], mu16[:], 1.0 / HD)
                    vector.tensor_scalar_mul(m216[:], m216[:], 1.0 / HD)
                    vfence(vector)
                    vector.tensor_tensor(var16[:], mu16[:], mu16[:], ALU.mult)
                    vfence(vector)
                    vector.tensor_tensor(var16[:], m216[:], var16[:], ALU.subtract)
                    vector.mark(SV, 1)
                    vector.wait_ge(SS, S2(j, 2))
                    mub = mu16[:].unsqueeze(2).broadcast_to([128, 16, HD])
                    rkb = rk16[:].unsqueeze(2).broadcast_to([128, 16, HD])
                    wrb = wrep8[:].unsqueeze(1).broadcast_to([128, 16, HD])
                    brb = brep8[:].unsqueeze(1).broadcast_to([128, 16, HD])
                    t2v = s_t2q[:].rearrange("p (g s) -> p g s", s=HD)
                    vector.tensor_tensor(t2v, p_q[pp][:].rearrange("p (g s) -> p g s", s=HD), mub, ALU.subtract)
                    vfence(vector)
                    vector.tensor_tensor(t2v, t2v, rkb, ALU.mult)
                    vfence(vector)
                    vector.tensor_tensor(t2v, t2v, wrb, ALU.mult)
                    vfence(vector)
                    vector.tensor_tensor(s_qtl[pp][:].rearrange("p (g s) -> p g s", s=HD), t2v, brb, ALU.add)
                    vector.mark(SV, 1)
                    vector.wait_ge(SS, S2(j, 3))
                    vector.reduce_sum(nrm16[:], scr32[:].rearrange("p (g s) -> p g s", s=HD), axis=AX)
                    vector.mark(SV, 1)
                    vector.wait_ge(SS, S2(j, 4))
                    vector.tensor_scalar(nb128[:], nrm16[:], -8.0, None, ALU.mult)
                    vector.mark(SV, 1)
                    vector.wait_ge(ST, T2tr(j))
                    vector.tensor_copy(s_negB[:, j * 128:(j + 1) * 128], p_nb[:])
                    vector.tensor_copy(s_qT[:, 0:4, j * 128:(j + 1) * 128],
                                       p_qt[:, 0:512].rearrange("p (c t) -> p c t", t=128))
                    vector.tensor_copy(s_qT[:, 4:8, j * 128:(j + 1) * 128],
                                       p_qt[:, 512:1024].rearrange("p (c t) -> p c t", t=128))
                    vector.mark(SV, 1)
                    if j + 2 < NJ:
                        emit_cast(j + 2)
                    else:
                        vector.memset(eps[:], LN_EPS)
                        vector.mark(SV, 1)
                vector.wait_ge(SV, V2ev(NJ - 1) + 1)
                vector.sem_inc(BAR, 1)

            @blk.tensor
            def _(tensor):
                tensor = _W(tensor)
                tensor.wait_ge(BAR, b2)

                def emit_q(j):
                    pp = j % 2
                    tensor.wait_ge(SV, V2cast(j))
                    if j >= 2:
                        tensor.wait_ge(SV, V2ev(j - 2))
                    mm = None
                    for ec in range(8):
                        st, sp = (ec == 0), (ec == 7)
                        xb = s_xtb[pp][:, ec, :]
                        tensor.matmul(p_q[pp][:, 0:512], xb, s_wqT[:, ec, 0:512], start=st, stop=sp)
                        mm = tensor.matmul(p_q[pp][:, 512:1024], xb, s_wqT[:, ec, 512:1024], start=st, stop=sp)
                    mm.then_inc(ST, 1)

                def emit_qtr(j):
                    pp = j % 2
                    tensor.wait_ge(SV, V2nb(j))
                    tensor.matmul(p_nb[:], nb128[:], ident[:], start=True, stop=True)
                    mm = None
                    for c in range(8):
                        mm = tensor.matmul(p_qt[:, c * 128:(c + 1) * 128],
                                           s_qtl[pp][:, c * 128:(c + 1) * 128], ident[:],
                                           start=(c % 4 == 0), stop=(c % 4 == 3))
                    mm.then_inc(ST, 1)

                emit_q(0)
                for j in range(1, NJ):
                    emit_q(j)
                    emit_qtr(j - 1)
                emit_qtr(NJ - 1)

            @blk.scalar
            def _(scalar):
                scalar = _W(scalar)
                scalar.wait_ge(BAR, b2)
                for j in range(NJ):
                    pp = j % 2
                    scalar.wait_ge(ST, T2kv(j))
                    if j >= 1:
                        scalar.wait_ge(SV, V2nrm(j - 1))
                    scalar.activation(scr32[:], p_q[pp][:], AF.Square)
                    scalar.mark(SS, 1)
                    scalar.wait_ge(SV, V2var(j))
                    scalar.activation(rk16[:], var16[:], AF.Ln, bias=eps[:])
                    scalar.activation(rk16[:], rk16[:], AF.Exp, scale=mhalf[:])
                    scalar.mark(SS, 1)
                    scalar.wait_ge(SV, V2app(j))
                    scalar.activation(scr32[:], s_qtl[pp][:], AF.Square)
                    scalar.mark(SS, 1)
                    scalar.wait_ge(SV, V2nrm(j))
                    scalar.activation(nrm16[:], nrm16[:], AF.Ln, bias=tiny[:])
                    scalar.activation(nrm16[:], nrm16[:], AF.Exp, scale=half[:])
                    scalar.mark(SS, 1)
                scalar.wait_ge(SS, S2(NJ - 1, 4))
                scalar.sem_inc(BAR, 1)

            @blk.gpsimd
            def _(gpsimd):
                gpsimd = _W(gpsimd)
                gpsimd.wait_ge(BAR, b2)
                gpsimd.sem_inc(BAR, 1)

        C["t"] = t2 + 2 * NJ
        C["v"] = v2 + 10 + 6 * NJ
        C["s"] = s2_ + 4 * NJ
        C["bar"] = b2 + 4

        if upto <= 2:
            stub = contextlib.ExitStack()
            s_z = stub.enter_context(nc.sbuf_tensor("s_zstub", [128, 1024], F32, side="right"))
            with nc.Block("stub") as blk:
                @blk.gpsimd
                def _(gpsimd):
                    gpsimd = _W(gpsimd)
                    gpsimd.wait_ge(BAR, C["bar"])
                    gpsimd.memset(s_z[:], 0.0)
                    for j in range(NJ):
                        if j >= 2:
                            gpsimd.wait_ge(SO[j % 2], 16 * (j // 2))
                        gpsimd.dma_start(out=o_out[j * 128:(j + 1) * 128, :], in_=s_z[:]).then_inc(SO[j % 2], 16)
                    gpsimd.wait_ge(SO[0], 16 * 8)
                    gpsimd.wait_ge(SO[1], 16 * 8)
                @blk.sync
                def _(sync):
                    sync = _W(sync)
                    sync.sem_inc(BAR, 1)
                @blk.vector
                def _(vector):
                    vector = _W(vector)
                    vector.sem_inc(BAR, 1)
                @blk.scalar
                def _(scalar):
                    scalar = _W(scalar)
                    scalar.sem_inc(BAR, 1)
            stub.close()
            for _st in reversed(all_stacks):
                _st.close()
            return nc

        ppb.close()
        p1b.close()
        p1.close()
        p0.close()

        ao_es = contextlib.ExitStack(); all_stacks.append(ao_es)
        s_aoT = ao_es.enter_context(nc.sbuf_tensor("s_aoT", [128, 8, 2048], BF16, side="right"))
        p2 = contextlib.ExitStack(); all_stacks.append(p2)
        s_khb = [p2.enter_context(nc.sbuf_tensor(f"s_khb{i}", [128, T], BF16, side="right")) for i in range(2)]
        s_qhb = [p2.enter_context(nc.sbuf_tensor(f"s_qhb{i}", [128, 2048], BF16, side="right")) for i in range(2)]
        s_at = [p2.enter_context(nc.sbuf_tensor(f"s_at{i}", [128, 1024], BF16, side="right")) for i in range(4)]
        s_rrep = p2.enter_context(nc.sbuf_tensor("s_rrep", [64, 2048], F32, side="right"))
        s_tmpO = p2.enter_context(nc.sbuf_tensor("s_tmpO", [64, 2048], BF16, side="right"))
        s_l = p2.enter_context(nc.sbuf_tensor("s_l", [1, 2048], F32, side="right"))
        s_r = p2.enter_context(nc.sbuf_tensor("s_r", [1, 2048], F32, side="right"))
        pp2 = contextlib.ExitStack(); all_stacks.append(pp2)
        p_o = pp2.enter_context(nc.psum_tensor("p_o", [128, 2048], F32))
        p_s = [pp2.enter_context(nc.psum_tensor(f"p_s{i}", [128, 1024], F32)) for i in range(2)]

        t3, av3, v3_, s3_, b3 = C["t"], C["av"], C["v"], C["s"], C["bar"]
        G2 = lambda h, k: h * NST + k
        VL = lambda h: v3_ + 1 + 2 * h + 1
        VN = lambda h: v3_ + 1 + 2 * h + 2

        with nc.Block("p2") as blk:
            @blk.sync
            def _(sync):
                sync = _W(sync)
                sync.wait_ge(BAR, b3)
                for h in range(H):
                    sync.wait_ge(SXA[h % 2], 16 * (24 + h // 2))
                    if h >= 2:
                        sync.wait_ge(ST, t3 + G2(h - 1, 0))
                    sync.dma_start(out=s_khb[h % 2][0:64, :],
                                   in_=d_kT[h // 2, (h % 2) * 64:(h % 2) * 64 + 64, :]).then_inc(SXA[h % 2], 16)
                sync.sem_inc(BAR, 1)

            @blk.gpsimd
            def _(gpsimd):
                gpsimd = _W(gpsimd)
                gpsimd.wait_ge(BAR, b3)
                for h in range(H):
                    if h >= 2:
                        gpsimd.wait_ge(SGB[h % 2], 32 * (h // 2))
                        gpsimd.wait_ge(ST, t3 + G2(h - 1, 0))
                    gpsimd.dma_start(out=s_qhb[h % 2][0:64, :],
                                     in_=s_qT[(h % 2) * 64:(h % 2) * 64 + 64, h // 2, :]).then_inc(SGB[h % 2], 16)
                    gpsimd.dma_start(out=s_qhb[h % 2][64:65, :],
                                     in_=s_negB[h:h + 1, :]).then_inc(SGB[h % 2], 16)
                    gpsimd.wait_ge(SV, VL(h))
                    if h >= 1:
                        gpsimd.wait_ge(SRB[(h - 1) % 2], 32 * ((h - 1) // 2 + 1))
                    if h >= 2:
                        gpsimd.wait_ge(SRB[h % 2], 32 * (h // 2))
                        gpsimd.wait_ge(SV, VN(h - 2))
                    gpsimd.dma_start(out=d_r, in_=s_r[:]).then_inc(SRB[h % 2], 16)
                    gpsimd.wait_ge(SRB[h % 2], 32 * (h // 2) + 16)
                    gpsimd.dma_start(out=s_rrep[:], in_=d_r.broadcast_to([64, TO])).then_inc(SRB[h % 2], 16)
                    if h % 2 == 1:
                        o = (h - 1) // 2
                        if o >= 2:
                            gpsimd.wait_ge(SGC[o % 2], 16 * (o // 2))
                        gpsimd.wait_ge(SV, VN(h))
                        gpsimd.dma_start(out=s_aoT[64:128, h // 2, :], in_=s_tmpO[:]).then_inc(SGC[o % 2], 16)
                gpsimd.wait_ge(SGC[0], 16 * 4)
                gpsimd.wait_ge(SGC[1], 16 * 4)
                gpsimd.sem_inc(BAR, 1)

            @blk.vector
            def _(vector):
                vector = _W(vector)
                vector.wait_ge(BAR, b3)
                vector.memset(s_khb[0][64:65, :], 1.0)
                vector.memset(s_khb[1][64:65, :], 1.0)
                vector.mark(SV, 1)
                for h in range(H):
                    for k in MASKED:
                        kb, c, r0, r1, mask = STEPS[k]
                        g = G2(h, k)
                        jm, tsel = mask
                        off = jm * 128 - r0
                        vector.wait_ge(SS, s3_ + g + 1)
                        vector.tensor_tensor(s_at[g % 4][:, off:off + 128],
                                             s_at[g % 4][:, off:off + 128],
                                             masks[:, jm, tsel, :], ALU.mult)
                        vector.mark(SM, 1)
                    vector.wait_ge(SAV, av3 + G2(h, NST))
                    vector.tensor_scalar(s_l[:, 0:1024], p_o[64:65, 0:1024], 1e-35, None, ALU.max)
                    vector.tensor_scalar(s_l[:, 1024:2048], p_o[64:65, 1024:2048], 1e-35, None, ALU.max)
                    vfence(vector)
                    vector.reciprocal(s_r[:], s_l[:])
                    vector.mark(SV, 1)
                    vector.wait_ge(SRB[h % 2], 32 * (h // 2 + 1))
                    if h % 2 == 0:
                        dst = s_aoT[0:64, h // 2, :]
                    else:
                        o = (h - 1) // 2
                        if o >= 1:
                            vector.wait_ge(SGC[(o - 1) % 2], 16 * ((o - 1) // 2 + 1))
                        dst = s_tmpO[:]
                    vector.tensor_tensor(dst, p_o[0:64, :], s_rrep[:], ALU.mult)
                    vector.mark(SV, 1)
                vector.wait_ge(SV, VN(H - 1))
                vector.sem_inc(BAR, 1)

            @blk.tensor
            def _(tensor):
                tensor = _W(tensor)
                tensor.wait_ge(BAR, b3)
                tensor.wait_ge(SV, v3_ + 1)

                def emit_s(h, k):
                    kb, c, r0, r1, mask = STEPS[k]
                    g = G2(h, k)
                    pp = g % 2
                    if g >= 2:
                        tensor.wait_ge(SS, s3_ + g - 1)
                    lo = c * 1024
                    mm = None
                    for (a, b_) in split512(r0 - lo, r1 - lo):
                        mm = tensor.matmul(p_s[pp][:, a:b_],
                                           s_khb[h % 2][0:65, kb * 128:(kb + 1) * 128],
                                           s_qhb[h % 2][0:65, lo + a:lo + b_],
                                           start=True, stop=True)
                    mm.then_inc(ST, 1)

                def emit_av(h, k):
                    kb, c, r0, r1, mask = STEPS[k]
                    g = G2(h, k)
                    if mask is not None:
                        tensor.wait_ge(SM, h * NMH + MPREF[k])
                    else:
                        tensor.wait_ge(SS, s3_ + g + 1)
                    if kb == 0 and h >= 1:
                        tensor.wait_ge(SV, VN(h - 1))
                    mm = None
                    for (a, b_) in split512(r0, r1):
                        bank = a // 512
                        mm = tensor.matmul(p_o[0:65, a:b_],
                                           vhat[:, kb * VROW + h * 65: kb * VROW + h * 65 + 65],
                                           s_at[g % 4][:, a - r0:b_ - r0],
                                           start=(kb == 0), stop=(kb == 8 * bank + 7))
                    mm.then_inc(SAV, 1)

                for h in range(H):
                    tensor.wait_ge(SXA[h % 2], 16 * (24 + h // 2 + 1))
                    tensor.wait_ge(SGB[h % 2], 32 * (h // 2 + 1))
                    for k in range(NST):
                        emit_s(h, k)
                        if k >= 2:
                            emit_av(h, k - 2)
                    emit_av(h, NST - 2)
                    emit_av(h, NST - 1)

            @blk.scalar
            def _(scalar):
                scalar = _W(scalar)
                scalar.wait_ge(BAR, b3)
                for h in range(H):
                    for k, (kb, c, r0, r1, mask) in enumerate(STEPS):
                        g = G2(h, k)
                        pp = g % 2
                        scalar.wait_ge(ST, t3 + g + 1)
                        if g >= 4:
                            scalar.wait_ge(SAV, av3 + g - 3)
                        scalar.activation(s_at[g % 4][:, 0:r1 - r0],
                                          p_s[pp][:, r0 - c * 1024:r1 - c * 1024], AF.Exp)
                        scalar.mark(SS, 1)
                scalar.wait_ge(SS, s3_ + H * NST)
                scalar.sem_inc(BAR, 1)

        C["t"] = t3 + H * NST
        C["av"] = av3 + H * NST
        C["v"] = v3_ + 1 + 2 * H
        C["s"] = s3_ + H * NST
        C["bar"] = b3 + 4

        if upto <= 3:
            stub = contextlib.ExitStack()
            s_z = stub.enter_context(nc.sbuf_tensor("s_zstub", [128, 1024], F32, side="right"))
            with nc.Block("stub") as blk:
                @blk.gpsimd
                def _(gpsimd):
                    gpsimd = _W(gpsimd)
                    gpsimd.wait_ge(BAR, C["bar"])
                    gpsimd.memset(s_z[:], 0.0)
                    for j in range(NJ):
                        if j >= 2:
                            gpsimd.wait_ge(SO[j % 2], 16 * (j // 2))
                        gpsimd.dma_start(out=o_out[j * 128:(j + 1) * 128, :], in_=s_z[:]).then_inc(SO[j % 2], 16)
                    gpsimd.wait_ge(SO[0], 16 * 8)
                    gpsimd.wait_ge(SO[1], 16 * 8)
                @blk.sync
                def _(sync):
                    sync = _W(sync)
                    sync.sem_inc(BAR, 1)
                @blk.vector
                def _(vector):
                    vector = _W(vector)
                    vector.sem_inc(BAR, 1)
                @blk.scalar
                def _(scalar):
                    scalar = _W(scalar)
                    scalar.sem_inc(BAR, 1)
            stub.close()
            for _st in reversed(all_stacks):
                _st.close()
            return nc

        pp2.close()
        p2.close()
        qT_es.close()
        vh_es.close()

        p3 = contextlib.ExitStack(); all_stacks.append(p3)
        scr32 = p3.enter_context(nc.sbuf_tensor("scr32_b", [128, 1024], F32, side="right"))
        scr32b = p3.enter_context(nc.sbuf_tensor("scr32_b2", [128, 1024], F32, side="right"))
        s_u = p3.enter_context(nc.sbuf_tensor("s_u_b", [128, 1024], F32, side="right"))
        s_u2 = p3.enter_context(nc.sbuf_tensor("s_u_b2", [128, 1024], F32, side="right"))
        mu2c = p3.enter_context(nc.sbuf_tensor("mu2c", [128, 2], F32, side="right"))
        m22c = p3.enter_context(nc.sbuf_tensor("m22c", [128, 2], F32, side="right"))
        var2c = p3.enter_context(nc.sbuf_tensor("var2c", [128, 2], F32, side="right"))
        rs2c = p3.enter_context(nc.sbuf_tensor("rs2c", [128, 2], F32, side="right"))
        s_wst = [p3.enter_context(nc.sbuf_tensor(f"s_wstb{i}", [128, 1024], F32, side="right")) for i in range(2)]
        s_wuT = p3.enter_context(nc.sbuf_tensor("s_wuT", [128, 8, 1024], BF16, side="right"))
        s_xqb = [p3.enter_context(nc.sbuf_tensor(f"s_xqb{i}", [128, 1024], F32, side="right")) for i in range(2)]
        s_y = [p3.enter_context(nc.sbuf_tensor(f"s_y{i}", [128, 1024], F32, side="right")) for i in range(2)]
        s_yb = [p3.enter_context(nc.sbuf_tensor(f"s_yb{i}", [128, 1024], BF16, side="right")) for i in range(2)]
        yT_es = contextlib.ExitStack(); all_stacks.append(yT_es)
        s_yT = yT_es.enter_context(nc.sbuf_tensor("s_yT", [128, 8, 2048], BF16))
        pp3 = contextlib.ExitStack(); all_stacks.append(pp3)
        p_z = [pp3.enter_context(nc.psum_tensor(f"p_z{i}", [128, 1024], F32)) for i in range(2)]
        p_yt = pp3.enter_context(nc.psum_tensor("p_yt", [128, 1024], F32))
        p_yt2 = pp3.enter_context(nc.psum_tensor("p_yt2", [128, 1024], F32))

        t4, v4, s4_, b4 = C["t"], C["v"], C["s"], C["bar"]
        NP3 = NJ // 2
        V4 = lambda i, k: v4 + 8 + 5 * i + k
        T4kv = lambda j: t4 + (1 if j == 0 else 2 * j)
        T4tr = lambda j: t4 + (2 * j + 3 if j <= NJ - 2 else 2 * NJ)
        S4 = lambda i, k: s4_ + 2 * i + k

        with nc.Block("p3") as blk:
            @blk.sync
            def _(sync):
                sync = _W(sync)
                sync.wait_ge(BAR, b4)
                for k in range(8):
                    sync.wait_ge(SW[k % 2], 16 * (12 + k // 2))
                    if k >= 2:
                        sync.wait_ge(SV, v4 + k - 1)
                    sync.dma_start(out=s_wst[k % 2][:], in_=i_wuT[k * 128:(k + 1) * 128, :]).then_inc(SW[k % 2], 16)
                for j in range(NJ):
                    sync.wait_ge(SXA[j % 2], 16 * (32 + j // 2))
                    if j >= 2:
                        sync.wait_ge(SV, V4(j // 2 - 1, 1))
                    sync.dma_start(out=s_xqb[j % 2][:], in_=i_xq[j * 128:(j + 1) * 128, :]).then_inc(SXA[j % 2], 16)
                sync.wait_ge(SXA[0], 16 * 40)
                sync.wait_ge(SXA[1], 16 * 40)
                sync.sem_inc(BAR, 1)

            @blk.vector
            def _(vector):
                vector = _W(vector)
                vector.wait_ge(BAR, b4)
                for k in range(8):
                    vector.wait_ge(SW[k % 2], 16 * (12 + k // 2 + 1))
                    vector.tensor_copy(s_wuT[:, k, :], s_wst[k % 2][:])
                    vector.mark(SV, 1)
                for i in range(NP3):
                    ja, jb = 2 * i, 2 * i + 1
                    vector.wait_ge(ST, T4kv(jb))
                    vector.wait_ge(SXA[1], 16 * (32 + jb // 2 + 1))
                    vector.wait_ge(SXA[0], 16 * (32 + ja // 2 + 1))
                    vector.tensor_tensor(s_u[:], p_z[0][:], s_xqb[0][:], ALU.add)
                    vector.tensor_tensor(s_u2[:], p_z[1][:], s_xqb[1][:], ALU.add)
                    vfence(vector)
                    vector.reduce_sum(mu2c[:, 0:1], s_u[:], axis=AX)
                    vector.reduce_sum(mu2c[:, 1:2], s_u2[:], axis=AX)
                    vfence(vector)
                    vector.tensor_scalar_mul(mu2c[:], mu2c[:], 1.0 / D)
                    vector.mark(SV, 1)
                    vector.wait_ge(SS, S4(i, 1))
                    vector.reduce_sum(m22c[:, 0:1], scr32[:], axis=AX)
                    vector.reduce_sum(m22c[:, 1:2], scr32b[:], axis=AX)
                    vfence(vector)
                    vector.tensor_scalar_mul(m22c[:], m22c[:], 1.0 / D)
                    vfence(vector)
                    vector.tensor_tensor(var2c[:], mu2c[:], mu2c[:], ALU.mult)
                    vfence(vector)
                    vector.tensor_tensor(var2c[:], m22c[:], var2c[:], ALU.subtract)
                    vector.mark(SV, 1)
                    vector.wait_ge(SS, S4(i, 2))
                    if i >= 1:
                        vector.wait_ge(SGA[0], 16 * (16 + i))
                        vector.wait_ge(SGA[1], 16 * (16 + i))
                    vector.tensor_scalar(s_y[0][:], s_u[:], mu2c[:, 0:1], rs2c[:, 0:1], ALU.subtract, ALU.mult)
                    vector.tensor_scalar(s_y[1][:], s_u2[:], mu2c[:, 1:2], rs2c[:, 1:2], ALU.subtract, ALU.mult)
                    vfence(vector)
                    vector.tensor_copy(s_yb[0][:], s_y[0][:])
                    vector.tensor_copy(s_yb[1][:], s_y[1][:])
                    vector.mark(SV, 1)
                    vector.wait_ge(ST, T4tr(ja))
                    vector.tensor_copy(s_yT[:, 0:4, ja * 128:(ja + 1) * 128],
                                       p_yt[:, 0:512].rearrange("p (c t) -> p c t", t=128))
                    vector.tensor_copy(s_yT[:, 4:8, ja * 128:(ja + 1) * 128],
                                       p_yt[:, 512:1024].rearrange("p (c t) -> p c t", t=128))
                    vector.mark(SV, 1)
                    vector.wait_ge(ST, T4tr(jb))
                    vector.tensor_copy(s_yT[:, 0:4, jb * 128:(jb + 1) * 128],
                                       p_yt2[:, 0:512].rearrange("p (c t) -> p c t", t=128))
                    vector.tensor_copy(s_yT[:, 4:8, jb * 128:(jb + 1) * 128],
                                       p_yt2[:, 512:1024].rearrange("p (c t) -> p c t", t=128))
                    vector.mark(SV, 1)
                vector.wait_ge(SV, V4(NP3 - 1, 5))
                vector.sem_inc(BAR, 1)

            @blk.tensor
            def _(tensor):
                tensor = _W(tensor)
                tensor.wait_ge(BAR, b4)
                tensor.wait_ge(SV, v4 + 8)

                def emit_wu(j):
                    pp = j % 2
                    if j >= 2:
                        tensor.wait_ge(SV, V4(j // 2 - 1, 1))
                    mm = None
                    for ec in range(8):
                        st, sp = (ec == 0), (ec == 7)
                        ao = s_aoT[:, ec, j * 128:(j + 1) * 128]
                        tensor.matmul(p_z[pp][:, 0:512], ao, s_wuT[:, ec, 0:512], start=st, stop=sp)
                        mm = tensor.matmul(p_z[pp][:, 512:1024], ao, s_wuT[:, ec, 512:1024], start=st, stop=sp)
                    mm.then_inc(ST, 1)

                def emit_ytr(j):
                    i = j // 2
                    tensor.wait_ge(SV, V4(i, 3))
                    if i >= 1:
                        tensor.wait_ge(SV, V4(i - 1, 4 + (j % 2)))
                    dst = p_yt if j % 2 == 0 else p_yt2
                    mm = None
                    for c in range(8):
                        mm = tensor.matmul(dst[:, c * 128:(c + 1) * 128],
                                           s_yb[j % 2][:, c * 128:(c + 1) * 128], ident[:],
                                           start=(c % 4 == 0), stop=(c % 4 == 3))
                    mm.then_inc(ST, 1)

                emit_wu(0)
                for j in range(1, NJ):
                    emit_wu(j)
                    emit_ytr(j - 1)
                emit_ytr(NJ - 1)

            @blk.scalar
            def _(scalar):
                scalar = _W(scalar)
                scalar.wait_ge(BAR, b4)
                for i in range(NP3):
                    scalar.wait_ge(SV, V4(i, 1))
                    scalar.activation(scr32[:], s_u[:], AF.Square)
                    scalar.activation(scr32b[:], s_u2[:], AF.Square)
                    scalar.mark(SS, 1)
                    scalar.wait_ge(SV, V4(i, 2))
                    scalar.activation(rs2c[:], var2c[:], AF.Ln, bias=eps[:])
                    scalar.activation(rs2c[:], rs2c[:], AF.Exp, scale=mhalf[:])
                    scalar.mark(SS, 1)
                scalar.wait_ge(SS, s4_ + 2 * NP3)
                scalar.sem_inc(BAR, 1)

            @blk.gpsimd
            def _(gpsimd):
                gpsimd = _W(gpsimd)
                gpsimd.wait_ge(BAR, b4)
                for i in range(NP3):
                    gpsimd.wait_ge(SV, V4(i, 3))
                    if i >= 1:
                        gpsimd.wait_ge(SGA[0], 16 * (16 + i))
                    gpsimd.dma_start(out=d_y[2 * i], in_=s_y[0][:]).then_inc(SGA[0], 16)
                    if i >= 1:
                        gpsimd.wait_ge(SGA[1], 16 * (16 + i))
                    gpsimd.dma_start(out=d_y[2 * i + 1], in_=s_y[1][:]).then_inc(SGA[1], 16)
                gpsimd.wait_ge(SGA[0], 16 * (16 + NP3))
                gpsimd.wait_ge(SGA[1], 16 * (16 + NP3))
                gpsimd.sem_inc(BAR, 1)

                C["t"] = t4 + 2 * NJ
        C["v"] = v4 + 8 + 5 * (NJ // 2)
        C["s"] = s4_ + 2 * (NJ // 2)
        C["bar"] = b4 + 4

        if upto <= 4:
            stub = contextlib.ExitStack()
            s_z = stub.enter_context(nc.sbuf_tensor("s_zstub", [128, 1024], F32, side="right"))
            with nc.Block("stub") as blk:
                @blk.gpsimd
                def _(gpsimd):
                    gpsimd = _W(gpsimd)
                    gpsimd.wait_ge(BAR, C["bar"])
                    gpsimd.memset(s_z[:], 0.0)
                    for j in range(NJ):
                        if j >= 2:
                            gpsimd.wait_ge(SO[j % 2], 16 * (j // 2))
                        gpsimd.dma_start(out=o_out[j * 128:(j + 1) * 128, :], in_=s_z[:]).then_inc(SO[j % 2], 16)
                    gpsimd.wait_ge(SO[0], 16 * 8)
                    gpsimd.wait_ge(SO[1], 16 * 8)
                @blk.sync
                def _(sync):
                    sync = _W(sync)
                    sync.sem_inc(BAR, 1)
                @blk.vector
                def _(vector):
                    vector = _W(vector)
                    vector.sem_inc(BAR, 1)
                @blk.scalar
                def _(scalar):
                    scalar = _W(scalar)
                    scalar.sem_inc(BAR, 1)
            stub.close()
            for _st in reversed(all_stacks):
                _st.close()
            return nc

        pp3.close()
        p3.close()
        ao_es.close()

        a1_es = contextlib.ExitStack(); all_stacks.append(a1_es)
        s_a1 = a1_es.enter_context(nc.sbuf_tensor("s_a1", [128, 32, 2048], BF16, side="right"))
        p4 = contextlib.ExitStack(); all_stacks.append(p4)
        s_k0 = [p4.enter_context(nc.sbuf_tensor(f"s_k0_{i}", [128, 16, 128], U8, side="right")) for i in range(2)]
        s_c0 = [p4.enter_context(nc.sbuf_tensor(f"s_c0_{i}", [128, 8, 128], BF16, side="right")) for i in range(2)]
        pp4 = contextlib.ExitStack(); all_stacks.append(pp4)
        p_h1 = [pp4.enter_context(nc.psum_tensor(f"p_h1_{i}", [128, 2048], F32)) for i in range(2)]

        t5, v5, s5_, b5 = C["t"], C["v"], C["s"], C["bar"]
        with nc.Block("p4") as blk:
            @blk.sync
            def _(sync):
                sync = _W(sync)
                sync.wait_ge(BAR, b5)
                for fc in range(32):
                    sync.wait_ge(SXA[fc % 2], 16 * (40 + fc // 2))
                    if fc >= 2:
                        sync.wait_ge(SV, v5 + fc - 1)
                    sync.dma_start(
                        out=s_k0[fc % 2][:],
                        in_=i_k0[:, fc * 128:(fc + 1) * 128].rearrange("(c p) f -> p c f", p=128),
                    ).then_inc(SXA[fc % 2], 16)
                sync.wait_ge(SXA[0], 16 * 56)
                sync.wait_ge(SXA[1], 16 * 56)
                sync.sem_inc(BAR, 1)

            @blk.vector
            def _(vector):
                vector = _W(vector)
                vector.wait_ge(BAR, b5)
                for fc in range(32):
                    vector.wait_ge(SXA[fc % 2], 16 * (40 + fc // 2 + 1))
                    if fc >= 2:
                        vector.wait_ge(ST, t5 + fc - 1)
                    vector.tensor_tensor(s_c0[fc % 2][:], s_k0[fc % 2][:, 0:8, :],
                                         s_k0[fc % 2][:, 8:16, :], ALU.subtract)
                    vector.mark(SV, 1)
                vector.wait_ge(SV, v5 + 32)
                vector.sem_inc(BAR, 1)

            @blk.tensor
            def _(tensor):
                tensor = _W(tensor)
                tensor.wait_ge(BAR, b5)
                for fc in range(32):
                    pp = fc % 2
                    tensor.wait_ge(SV, v5 + fc + 1)
                    if fc >= 2:
                        tensor.wait_ge(SS, s5_ + fc - 1)
                    mm = None
                    for ec in range(8):
                        st, sp = (ec == 0), (ec == 7)
                        w = s_c0[pp][:, ec, :]
                        for q in range(4):
                            mm = tensor.matmul(p_h1[pp][:, q * 512:(q + 1) * 512], w,
                                               s_yT[:, ec, q * 512:(q + 1) * 512], start=st, stop=sp)
                    mm.then_inc(ST, 1)

            @blk.scalar
            def _(scalar):
                scalar = _W(scalar)
                scalar.wait_ge(BAR, b5)
                for fc in range(32):
                    pp = fc % 2
                    scalar.wait_ge(ST, t5 + fc + 1)
                    scalar.activation(s_a1[:, fc, 0:1024], p_h1[pp][:, 0:1024], AF.Relu, scale=sc2s0[:])
                    scalar.activation(s_a1[:, fc, 1024:2048], p_h1[pp][:, 1024:2048], AF.Relu, scale=sc2s0[:])
                    scalar.mark(SS, 1)
                scalar.wait_ge(SS, s5_ + 32)
                scalar.sem_inc(BAR, 1)

            @blk.gpsimd
            def _(gpsimd):
                gpsimd = _W(gpsimd)
                gpsimd.wait_ge(BAR, b5)
                gpsimd.sem_inc(BAR, 1)

        C["t"] = t5 + 32
        C["v"] = v5 + 32
        C["s"] = s5_ + 32
        C["bar"] = b5 + 4

        if upto <= 5:
            stub = contextlib.ExitStack()
            s_z = stub.enter_context(nc.sbuf_tensor("s_zstub", [128, 1024], F32, side="right"))
            with nc.Block("stub") as blk:
                @blk.gpsimd
                def _(gpsimd):
                    gpsimd = _W(gpsimd)
                    gpsimd.wait_ge(BAR, C["bar"])
                    gpsimd.memset(s_z[:], 0.0)
                    for j in range(NJ):
                        if j >= 2:
                            gpsimd.wait_ge(SO[j % 2], 16 * (j // 2))
                        gpsimd.dma_start(out=o_out[j * 128:(j + 1) * 128, :], in_=s_z[:]).then_inc(SO[j % 2], 16)
                    gpsimd.wait_ge(SO[0], 16 * 8)
                    gpsimd.wait_ge(SO[1], 16 * 8)
                @blk.sync
                def _(sync):
                    sync = _W(sync)
                    sync.sem_inc(BAR, 1)
                @blk.vector
                def _(vector):
                    vector = _W(vector)
                    vector.sem_inc(BAR, 1)
                @blk.scalar
                def _(scalar):
                    scalar = _W(scalar)
                    scalar.sem_inc(BAR, 1)
            stub.close()
            for _st in reversed(all_stacks):
                _st.close()
            return nc

        pp4.close()
        p4.close()
        yT_es.close()

        p5 = contextlib.ExitStack(); all_stacks.append(p5)
        s_k1 = [p5.enter_context(nc.sbuf_tensor(f"s_k1_{i}", [128, 32, 128], U8, side="right")) for i in range(2)]
        s_w1 = [p5.enter_context(nc.sbuf_tensor(f"s_w1_{i}", [128, 32, 128], BF16, side="right")) for i in range(2)]
        s_a2t = [p5.enter_context(nc.sbuf_tensor(f"s_a2t{i}", [128, 2048], BF16, side="right")) for i in range(2)]
        pp5 = contextlib.ExitStack(); all_stacks.append(pp5)
        p_h2 = [pp5.enter_context(nc.psum_tensor(f"p_h2_{i}", [128, 2048], F32)) for i in range(2)]

        t6, v6, s6_, b6 = C["t"], C["v"], C["s"], C["bar"]
        S6c = lambda fc: s6_ + (1 if fc == 0 else 2 * fc)
        S6r = lambda fc: s6_ + (2 * fc + 3 if fc <= 29 else (63 if fc == 30 else 64))
        with nc.Block("p5") as blk:
            @blk.sync
            def _(sync):
                sync = _W(sync)
                sync.wait_ge(BAR, b6)
                for fc in range(32):
                    sync.wait_ge(SXA[fc % 2], 16 * (56 + fc // 2))
                    if fc >= 2:
                        sync.wait_ge(SV, v6 + fc - 1)
                        sync.wait_ge(SS, S6c(fc - 2))
                    sync.dma_start(
                        out=s_k1[fc % 2][:],
                        in_=i_k1[:, fc * 128:(fc + 1) * 128].rearrange("(c p) f -> p c f", p=128),
                    ).then_inc(SXA[fc % 2], 16)
                sync.wait_ge(SXA[0], 16 * 72)
                sync.wait_ge(SXA[1], 16 * 72)
                sync.sem_inc(BAR, 1)

            @blk.vector
            def _(vector):
                vector = _W(vector)
                vector.wait_ge(BAR, b6)
                for fc in range(32):
                    vector.wait_ge(SXA[fc % 2], 16 * (56 + fc // 2 + 1))
                    if fc >= 2:
                        vector.wait_ge(ST, t6 + fc - 1)
                    vector.tensor_scalar(s_w1[fc % 2][:, 0:16, :], s_k1[fc % 2][:, 0:16, :],
                                         2.0, -1.0, ALU.mult, ALU.add)
                    vector.mark(SV, 1)
                vector.wait_ge(SV, v6 + 32)
                vector.sem_inc(BAR, 1)

            @blk.scalar
            def _(scalar):
                scalar = _W(scalar)
                scalar.wait_ge(BAR, b6)

                def emit_conv(fc):
                    pp = fc % 2
                    scalar.wait_ge(SXA[pp], 16 * (56 + fc // 2 + 1))
                    if fc >= 2:
                        scalar.wait_ge(ST, t6 + fc - 1)
                    scalar.sign(s_w1[pp][:, 16:32, :], s_k1[pp][:, 16:32, :], bias=mhalf[:])
                    scalar.mark(SS, 1)

                def emit_relu(fc):
                    pp = fc % 2
                    scalar.wait_ge(ST, t6 + fc + 1)
                    if fc >= 2:
                        scalar.wait_ge(SGA[pp], 16 * (24 + fc // 2))
                    scalar.activation(s_a2t[pp][:, 0:1024], p_h2[pp][:, 0:1024], AF.Relu, scale=sc2s1[:])
                    scalar.activation(s_a2t[pp][:, 1024:2048], p_h2[pp][:, 1024:2048], AF.Relu, scale=sc2s1[:])
                    scalar.mark(SS, 1)

                emit_conv(0)
                emit_conv(1)
                for fc in range(30):
                    emit_relu(fc)
                    emit_conv(fc + 2)
                emit_relu(30)
                emit_relu(31)
                scalar.wait_ge(SS, s6_ + 64)
                scalar.sem_inc(BAR, 1)

            @blk.tensor
            def _(tensor):
                tensor = _W(tensor)
                tensor.wait_ge(BAR, b6)
                for fc in range(32):
                    pp = fc % 2
                    tensor.wait_ge(SV, v6 + fc + 1)
                    tensor.wait_ge(SS, S6c(fc))
                    if fc >= 2:
                        tensor.wait_ge(SS, S6r(fc - 2))
                    mm = None
                    for f0c in range(32):
                        st, sp = (f0c == 0), (f0c == 31)
                        w = s_w1[pp][:, f0c, :]
                        for q in range(4):
                            mm = tensor.matmul(p_h2[pp][:, q * 512:(q + 1) * 512], w,
                                               s_a1[:, f0c, q * 512:(q + 1) * 512], start=st, stop=sp)
                    mm.then_inc(ST, 1)

            @blk.gpsimd
            def _(gpsimd):
                gpsimd = _W(gpsimd)
                gpsimd.wait_ge(BAR, b6)
                for fc in range(32):
                    gpsimd.wait_ge(SS, S6r(fc))
                    if fc >= 2:
                        gpsimd.wait_ge(SGA[fc % 2], 16 * (24 + fc // 2))
                    gpsimd.dma_start(out=d_a2[fc], in_=s_a2t[fc % 2][:]).then_inc(SGA[fc % 2], 16)
                gpsimd.wait_ge(SGA[0], 16 * 40)
                gpsimd.wait_ge(SGA[1], 16 * 40)
                gpsimd.sem_inc(BAR, 1)

        C["t"] = t6 + 32
        C["v"] = v6 + 32
        C["s"] = s6_ + 64
        C["bar"] = b6 + 4

        if upto <= 6:
            stub = contextlib.ExitStack()
            s_z = stub.enter_context(nc.sbuf_tensor("s_zstub", [128, 1024], F32, side="right"))
            with nc.Block("stub") as blk:
                @blk.gpsimd
                def _(gpsimd):
                    gpsimd = _W(gpsimd)
                    gpsimd.wait_ge(BAR, C["bar"])
                    gpsimd.memset(s_z[:], 0.0)
                    for j in range(NJ):
                        if j >= 2:
                            gpsimd.wait_ge(SO[j % 2], 16 * (j // 2))
                        gpsimd.dma_start(out=o_out[j * 128:(j + 1) * 128, :], in_=s_z[:]).then_inc(SO[j % 2], 16)
                    gpsimd.wait_ge(SO[0], 16 * 8)
                    gpsimd.wait_ge(SO[1], 16 * 8)
                @blk.sync
                def _(sync):
                    sync = _W(sync)
                    sync.sem_inc(BAR, 1)
                @blk.vector
                def _(vector):
                    vector = _W(vector)
                    vector.sem_inc(BAR, 1)
                @blk.scalar
                def _(scalar):
                    scalar = _W(scalar)
                    scalar.sem_inc(BAR, 1)
            stub.close()
            for _st in reversed(all_stacks):
                _st.close()
            return nc

        pp5.close()
        p5.close()
        a1_es.close()

        p6 = contextlib.ExitStack(); all_stacks.append(p6)
        scr32 = p6.enter_context(nc.sbuf_tensor("scr32_c", [128, 1024], F32, side="right"))
        s_u = p6.enter_context(nc.sbuf_tensor("s_u_c", [128, 1024], F32, side="right"))
        s_k2s = [p6.enter_context(nc.sbuf_tensor(f"s_k2s{i}", [128, 1024], U8, side="right")) for i in range(2)]
        s_k2b = p6.enter_context(nc.sbuf_tensor("s_k2b", [128, 32, 1024], BF16, side="right"))
        s_a2b = [p6.enter_context(nc.sbuf_tensor(f"s_a2b{i}", [128, 32, 128], BF16, side="right")) for i in range(2)]
        s_yr = [p6.enter_context(nc.sbuf_tensor(f"s_yr{i}", [128, 1024], F32, side="right")) for i in range(2)]
        s_ou = [p6.enter_context(nc.sbuf_tensor(f"s_ou{i}", [128, 1024], F32, side="right")) for i in range(2)]
        pp6 = contextlib.ExitStack(); all_stacks.append(pp6)
        p_h3 = [pp6.enter_context(nc.psum_tensor(f"p_h3_{i}", [128, 1024], F32)) for i in range(2)]

        t7, v7, s7_, b7 = C["t"], C["v"], C["s"], C["bar"]
        V7 = lambda j, k: v7 + 32 + 3 * j + k
        T7 = lambda j: t7 + j + 1
        S7 = lambda j, k: s7_ + 2 * j + k
        with nc.Block("p6") as blk:
            @blk.sync
            def _(sync):
                sync = _W(sync)
                sync.wait_ge(BAR, b7)
                for i in range(32):
                    sync.wait_ge(SXA[i % 2], 16 * (72 + i // 2))
                    if i >= 2:
                        sync.wait_ge(SV, v7 + i - 1)
                    sync.dma_start(out=s_k2s[i % 2][:], in_=i_k2[i * 128:(i + 1) * 128, :]).then_inc(SXA[i % 2], 16)
                for j in range(NJ):
                    if j >= 2:
                        sync.wait_ge(SXB[j % 2], 16 * (j // 2))
                        sync.wait_ge(ST, T7(j - 2))
                    sync.dma_start(
                        out=s_a2b[j % 2][:],
                        in_=d_a2[:, :, j * 128:(j + 1) * 128].rearrange("c p t -> p c t"),
                    ).then_inc(SXB[j % 2], 16)
                    if j >= 2:
                        sync.wait_ge(SXC[j % 2], 16 * (j // 2))
                        sync.wait_ge(SV, V7(j - 2, 1))
                    sync.dma_start(out=s_yr[j % 2][:], in_=d_y[j]).then_inc(SXC[j % 2], 16)

            @blk.vector
            def _(vector):
                vector = _W(vector)
                vector.wait_ge(BAR, b7)
                for i in range(32):
                    vector.wait_ge(SXA[i % 2], 16 * (72 + i // 2 + 1))
                    vector.tensor_scalar(s_k2b[:, i, :], s_k2s[i % 2][:], 2.0, -1.0, ALU.mult, ALU.add)
                    vector.mark(SV, 1)
                for j in range(NJ):
                    pp = j % 2
                    vector.wait_ge(ST, T7(j))
                    vector.wait_ge(SXC[pp], 16 * (j // 2 + 1))
                    vector.tensor_scalar(s_u[:], p_h3[pp][:], sc2s2[:], None, ALU.mult)
                    vfence(vector)
                    vector.tensor_tensor(s_u[:], s_u[:], s_yr[pp][:], ALU.add)
                    vfence(vector)
                    vector.reduce_sum(mu1[:], s_u[:], axis=AX)
                    vfence(vector)
                    vector.tensor_scalar_mul(mu1[:], mu1[:], 1.0 / D)
                    vector.mark(SV, 1)
                    vector.wait_ge(SS, S7(j, 1))
                    vector.reduce_sum(m21[:], scr32[:], axis=AX)
                    vfence(vector)
                    vector.tensor_scalar_mul(m21[:], m21[:], 1.0 / D)
                    vfence(vector)
                    vector.tensor_tensor(var1[:], mu1[:], mu1[:], ALU.mult)
                    vfence(vector)
                    vector.tensor_tensor(var1[:], m21[:], var1[:], ALU.subtract)
                    vector.mark(SV, 1)
                    vector.wait_ge(SS, S7(j, 2))
                    if j >= 2:
                        vector.wait_ge(SO[pp], 16 * (j // 2))
                    vector.tensor_scalar(s_ou[pp][:], s_u[:], mu1[:], rs1[:], ALU.subtract, ALU.mult)
                    vector.mark(SV, 1)

            @blk.tensor
            def _(tensor):
                tensor = _W(tensor)
                tensor.wait_ge(BAR, b7)
                tensor.wait_ge(SV, v7 + 32)
                for j in range(NJ):
                    pp = j % 2
                    tensor.wait_ge(SXB[pp], 16 * (j // 2 + 1))
                    if j >= 2:
                        tensor.wait_ge(SV, V7(j - 2, 1))
                    mm = None
                    for f1c in range(32):
                        st, sp = (f1c == 0), (f1c == 31)
                        a2 = s_a2b[pp][:, f1c, :]
                        tensor.matmul(p_h3[pp][:, 0:512], a2, s_k2b[:, f1c, 0:512], start=st, stop=sp)
                        mm = tensor.matmul(p_h3[pp][:, 512:1024], a2, s_k2b[:, f1c, 512:1024], start=st, stop=sp)
                    mm.then_inc(ST, 1)

            @blk.scalar
            def _(scalar):
                scalar = _W(scalar)
                scalar.wait_ge(BAR, b7)
                for j in range(NJ):
                    scalar.wait_ge(SV, V7(j, 1))
                    scalar.activation(scr32[:], s_u[:], AF.Square)
                    scalar.mark(SS, 1)
                    scalar.wait_ge(SV, V7(j, 2))
                    scalar.activation(rs1[:], var1[:], AF.Ln, bias=eps[:])
                    scalar.activation(rs1[:], rs1[:], AF.Exp, scale=mhalf[:])
                    scalar.mark(SS, 1)

            @blk.gpsimd
            def _(gpsimd):
                gpsimd = _W(gpsimd)
                gpsimd.wait_ge(BAR, b7)
                for j in range(NJ):
                    gpsimd.wait_ge(SV, V7(j, 3))
                    if j >= 2:
                        gpsimd.wait_ge(SO[j % 2], 16 * (j // 2))
                    gpsimd.dma_start(out=o_out[j * 128:(j + 1) * 128, :], in_=s_ou[j % 2][:]).then_inc(SO[j % 2], 16)
                gpsimd.wait_ge(SO[0], 16 * 8)
                gpsimd.wait_ge(SO[1], 16 * 8)

        if K_DBG:
            with nc.Block("dbg") as blk:
                @blk.gpsimd
                def _(gpsimd):
                    gpsimd = _W(gpsimd)
                    gpsimd.wait_ge(SO[0], 16 * 8)
                    gpsimd.wait_ge(SO[1], 16 * 8)
                    gpsimd.dma_start(out=o_dbgs, in_=s_dbgs[:]).then_inc(SGC[0], 16)
                    gpsimd.dma_start(out=o_dbgy, in_=d_y[:]).then_inc(SGC[0], 16)
                    gpsimd.dma_start(out=o_dbgk, in_=d_kT[:]).then_inc(SGC[0], 16)
                    gpsimd.dma_start(out=o_dbga, in_=d_a2[0:4]).then_inc(SGC[0], 16)
                    gpsimd.wait_ge(SGC[0], 16 * 4 + 16 * 4)
        pp6.close()
        p6.close()

    return nc


_NC_CACHE = None


def kernel(L0_attn_x, L0_attn_Wk, L0_attn_Wq, L0_attn_Wv, L0_attn_Wu,
           L0_attn_kln_w, L0_attn_kln_b, L0_attn_qln_w, L0_attn_qln_b,
           L0_ffn_k0, L0_ffn_k1, L0_ffn_k2, L0_ffn_s0, L0_ffn_s1, L0_ffn_s2):
    global _NC_CACHE
    x = np.ascontiguousarray(np.asarray(L0_attn_x, dtype=np.float32))
    wkT = np.ascontiguousarray(np.asarray(L0_attn_Wk, np.float32).T)
    wqT = np.ascontiguousarray(np.asarray(L0_attn_Wq, np.float32).T)
    wvT = np.ascontiguousarray(np.asarray(L0_attn_Wv, np.float32).T)
    wuT = np.ascontiguousarray(np.asarray(L0_attn_Wu, np.float32).T)
    klnw = np.ascontiguousarray(np.asarray(L0_attn_kln_w, np.float32).reshape(1, HD))
    qlnw = np.ascontiguousarray(np.asarray(L0_attn_qln_w, np.float32).reshape(1, HD))
    qlnb = np.ascontiguousarray(np.asarray(L0_attn_qln_b, np.float32).reshape(1, HD))
    k0 = np.ascontiguousarray(np.asarray(L0_ffn_k0)).view(np.uint8)
    k1 = np.ascontiguousarray(np.asarray(L0_ffn_k1)).view(np.uint8)
    k2 = np.ascontiguousarray(np.asarray(L0_ffn_k2)).view(np.uint8)
    s0 = np.ascontiguousarray(np.asarray(L0_ffn_s0, np.float32).reshape(1, 1))
    s1 = np.ascontiguousarray(np.asarray(L0_ffn_s1, np.float32).reshape(1, 1))
    s2 = np.ascontiguousarray(np.asarray(L0_ffn_s2, np.float32).reshape(1, 1))
    ident = np.eye(128).astype(bf16)

    in_maps = []
    rows_per_core = []
    q_i = np.arange(128)[:, None]
    k_i = np.arange(128)[None, :]
    for c in range(8):
        b, role = c // 2, c % 2
        obs = own_blocks(role)
        rows = np.concatenate([np.arange(g * 128, (g + 1) * 128) for g in obs])
        rows_per_core.append((b, rows))
        xb = x[b]
        xq = np.ascontiguousarray(xb[rows])
        m = np.zeros((NJ, 2, 128, 128), np.float32)
        for j in range(NJ):
            g = obs[j]
            for t in range(2):
                kb = 2 * j + t
                if kb == g:
                    m[j, t] = (q_i >= k_i).astype(np.float32).T
                elif kb < g:
                    m[j, t] = 1.0
        in_maps.append(dict(
            xT=np.ascontiguousarray(xb.T), xq=xq, xqT=np.ascontiguousarray(xq.T),
            wkT=wkT, wvT=wvT, wqT=wqT, wuT=wuT,
            klnw=klnw, qlnw=qlnw, qlnb=qlnb,
            k0=k0, k1=k1, k2=k2, s0=s0, s1=s1, s2=s2,
            ident=ident, masks=np.ascontiguousarray(m.astype(bf16)),
        ))

    if _NC_CACHE is None:
        _NC_CACHE = build_program()
    res = run_bass_kernel_spmd(_NC_CACHE, in_maps, core_ids=list(range(8)))
    out = np.empty((B, T, D), np.float32)
    for c in range(8):
        b, rows = rows_per_core[c]
        out[b][rows] = res.results[c]["out"]
    return out


# version 11
# speedup vs baseline: 1.0092x; 1.0092x over previous
import contextlib
import os

import numpy as np
import ml_dtypes

import concourse.bass as bass
import concourse.mybir as mybir
from concourse.bass_utils import run_bass_kernel_spmd

F32 = mybir.dt.float32
BF16 = mybir.dt.bfloat16
U8 = mybir.dt.uint8
AX = mybir.AxisListType.X
ALU = mybir.AluOpType
AF = mybir.ActivationFunctionType

B, T, D = 4, 4096, 1024
H, HD = 16, 64
NB = 32
NJ = 16
TO = NJ * 128
LN_EPS = 1e-5
VROW = H * 65

bf16 = ml_dtypes.bfloat16



class _W:

    def __init__(self, e):
        self._e = e
        self._last = None

    def __getattr__(self, name):
        f = getattr(self._e, name)
        if callable(f):
            def g(*a, **kw):
                r = f(*a, **kw)
                if hasattr(r, "then_inc") and name not in ("wait_ge", "sem_inc"):
                    self._last = r
                return r
            return g
        return f

    def mark(self, sem, n=1):
        assert self._last is not None, "mark() with no preceding instruction"
        self._last.then_inc(sem, n)
        self._last = None


def own_blocks(role):
    out = []
    for j in range(NJ):
        if role == 0:
            out.append(2 * j + (j & 1))
        else:
            out.append(2 * j + 1 - (j & 1))
    return out


def nq_of(kb):
    return 128 * (NJ - kb // 2)


def attn_steps():
    steps = []
    for kb in range(NB):
        qs = TO - nq_of(kb)
        jm = kb // 2
        for c in (0, 1):
            r0 = max(qs, c * 1024)
            r1 = (c + 1) * 1024
            if r0 >= r1:
                continue
            mask = (jm, kb - 2 * jm) if (jm * 128 >= r0 and jm * 128 < r1) else None
            steps.append((kb, c, r0, r1, mask))
    return steps


STEPS = attn_steps()
NST = len(STEPS)
MASKED = [i for i, s in enumerate(STEPS) if s[4] is not None]
MPREF = []
_c = 0
for _i in range(NST):
    if STEPS[_i][4] is not None:
        _c += 1
    MPREF.append(_c)
NMH = MPREF[-1]


def split512(a, b):
    out = []
    while a < b:
        nxt = min(b, (a // 512 + 1) * 512)
        out.append((a, nxt))
        a = nxt
    return out


def build_program(upto=99):
    K_LN = os.environ.get("K_LN", "1") == "1"
    K_APP = os.environ.get("K_APP", "1") == "1"
    K_RSTD = os.environ.get("K_RSTD", "1") == "1"
    K_SQ = os.environ.get("K_SQ", "1") == "1"
    K_RED = os.environ.get("K_RED", "1") == "1"
    K_TR = os.environ.get("K_TR", "1") == "1"
    nc = bass.Bass("TRN2", target_bir_lowering=False, debug=False, detect_race_conditions=False)

    i_xT = nc.dram_tensor("xT", [D, T], F32, kind="ExternalInput").ap()
    i_xq = nc.dram_tensor("xq", [TO, D], F32, kind="ExternalInput").ap()
    i_xqT = nc.dram_tensor("xqT", [D, TO], F32, kind="ExternalInput").ap()
    i_wkT = nc.dram_tensor("wkT", [D, D], F32, kind="ExternalInput").ap()
    i_wvT = nc.dram_tensor("wvT", [D, D], F32, kind="ExternalInput").ap()
    i_wqT = nc.dram_tensor("wqT", [D, D], F32, kind="ExternalInput").ap()
    i_wuT = nc.dram_tensor("wuT", [D, D], F32, kind="ExternalInput").ap()
    i_klnw = nc.dram_tensor("klnw", [1, HD], F32, kind="ExternalInput").ap()
    i_qlnw = nc.dram_tensor("qlnw", [1, HD], F32, kind="ExternalInput").ap()
    i_qlnb = nc.dram_tensor("qlnb", [1, HD], F32, kind="ExternalInput").ap()
    i_k0 = nc.dram_tensor("k0", [2 * D, 4096], U8, kind="ExternalInput").ap()
    i_k1 = nc.dram_tensor("k1", [4096, 4096], U8, kind="ExternalInput").ap()
    i_k2 = nc.dram_tensor("k2", [4096, D], U8, kind="ExternalInput").ap()
    i_s0 = nc.dram_tensor("s0", [1, 1], F32, kind="ExternalInput").ap()
    i_s1 = nc.dram_tensor("s1", [1, 1], F32, kind="ExternalInput").ap()
    i_s2 = nc.dram_tensor("s2", [1, 1], F32, kind="ExternalInput").ap()
    i_ident = nc.dram_tensor("ident", [128, 128], BF16, kind="ExternalInput").ap()
    i_masks = nc.dram_tensor("masks", [NJ, 2, 128, 128], BF16, kind="ExternalInput").ap()
    o_out = nc.dram_tensor("out", [TO, D], F32, kind="ExternalOutput").ap()
    K_DBG = os.environ.get("K_DBG", "0") == "1"
    if K_DBG:
        o_dbgs = nc.dram_tensor("dbgs", [128, 64], F32, kind="ExternalOutput").ap()
        o_dbgy = nc.dram_tensor("dbgy", [NJ, 128, D], F32, kind="ExternalOutput").ap()
        o_dbgk = nc.dram_tensor("dbgk", [8, 128, T], BF16, kind="ExternalOutput").ap()
        o_dbga = nc.dram_tensor("dbga", [4, 128, TO], BF16, kind="ExternalOutput").ap()
    d_kT = nc.dram_tensor("kT_scr", [8, 128, T], BF16, kind="Internal").ap()
    d_y = nc.dram_tensor("y_scr", [NJ, 128, D], F32, kind="Internal").ap()
    d_a2 = nc.dram_tensor("a2_scr", [32, 128, TO], BF16, kind="Internal").ap()
    d_r = nc.dram_tensor("r_scr", [1, TO], F32, kind="Internal").ap()

    es = contextlib.ExitStack()
    with es:
        s_dbgs = es.enter_context(nc.sbuf_tensor("s_dbgs", [128, 64], F32))
        ident = es.enter_context(nc.sbuf_tensor("sb_ident", [128, 128], BF16))
        masks = es.enter_context(nc.sbuf_tensor("sb_masks", [128, NJ, 2, 128], BF16))
        onesf = es.enter_context(nc.sbuf_tensor("onesf", [1, 128], F32))
        eps = es.enter_context(nc.sbuf_tensor("eps", [128, 1], F32))
        tiny = es.enter_context(nc.sbuf_tensor("tiny", [128, 1], F32))
        half = es.enter_context(nc.sbuf_tensor("half", [128, 1], F32))
        mhalf = es.enter_context(nc.sbuf_tensor("mhalf", [128, 1], F32))
        two_b = es.enter_context(nc.sbuf_tensor("two_b", [128, 1], F32))
        mone_b = es.enter_context(nc.sbuf_tensor("mone_b", [128, 1], F32))
        wrep8 = es.enter_context(nc.sbuf_tensor("wrep8", [128, HD], F32))
        brep8 = es.enter_context(nc.sbuf_tensor("brep8", [128, HD], F32))
        sc2s0 = es.enter_context(nc.sbuf_tensor("sc2s0", [128, 1], F32))
        sc2s1 = es.enter_context(nc.sbuf_tensor("sc2s1", [128, 1], F32))
        sc2s2 = es.enter_context(nc.sbuf_tensor("sc2s2", [128, 1], F32))
        row_a = es.enter_context(nc.sbuf_tensor("row_a", [1, HD], F32))
        row_b = es.enter_context(nc.sbuf_tensor("row_b", [1, HD], F32))
        klnw = es.enter_context(nc.sbuf_tensor("klnw_s", [1, HD], F32))
        qlnw = es.enter_context(nc.sbuf_tensor("qlnw_s", [1, HD], F32))
        qlnb = es.enter_context(nc.sbuf_tensor("qlnb_s", [1, HD], F32))
        sclr = es.enter_context(nc.sbuf_tensor("sclr", [1, 3], F32))
        mu16 = es.enter_context(nc.sbuf_tensor("mu16", [128, 16], F32))
        m216 = es.enter_context(nc.sbuf_tensor("m216", [128, 16], F32))
        var16 = es.enter_context(nc.sbuf_tensor("var16", [128, 16], F32))
        rk16 = es.enter_context(nc.sbuf_tensor("rk16", [128, 16], F32))
        nrm16 = es.enter_context(nc.sbuf_tensor("nrm16", [128, 16], F32))
        nb128 = es.enter_context(nc.sbuf_tensor("nb128", [128, 16], BF16))
        mu1 = es.enter_context(nc.sbuf_tensor("mu1", [128, 1], F32))
        m21 = es.enter_context(nc.sbuf_tensor("m21", [128, 1], F32))
        var1 = es.enter_context(nc.sbuf_tensor("var1", [128, 1], F32))
        rs1 = es.enter_context(nc.sbuf_tensor("rs1", [128, 1], F32))

        SC = es.enter_context(nc.semaphore("SC"))
        SW = [es.enter_context(nc.semaphore(f"SW{i}")) for i in range(2)]
        SXA = [es.enter_context(nc.semaphore(f"SXA{i}")) for i in range(2)]
        SXB = [es.enter_context(nc.semaphore(f"SXB{i}")) for i in range(2)]
        SXC = [es.enter_context(nc.semaphore(f"SXC{i}")) for i in range(2)]
        SGA = [es.enter_context(nc.semaphore(f"SGA{i}")) for i in range(2)]
        SGB = [es.enter_context(nc.semaphore(f"SGB{i}")) for i in range(2)]
        SGC = [es.enter_context(nc.semaphore(f"SGC{i}")) for i in range(2)]
        ST = es.enter_context(nc.semaphore("ST"))
        SAV = es.enter_context(nc.semaphore("SAV"))
        SRR = es.enter_context(nc.semaphore("SRR"))
        SRB = [es.enter_context(nc.semaphore(f"SRB{i}")) for i in range(2)]
        SM = es.enter_context(nc.semaphore("SM"))
        SV = es.enter_context(nc.semaphore("SV"))
        SS = es.enter_context(nc.semaphore("SS"))
        SO = [es.enter_context(nc.semaphore(f"SO{i}")) for i in range(2)]
        BAR = es.enter_context(nc.semaphore("BAR"))
        SVF = es.enter_context(nc.semaphore("SVF"))

        C = dict(t=0, av=0, rr=0, v=0, s=0, bar=0)
        _fence_n = [0]

        def vfence(vec):
            if vec._last is None:
                return
            _fence_n[0] += 1
            vec._last.then_inc(SVF, 1)
            vec._last = None
            vec.wait_ge(SVF, _fence_n[0])
        all_stacks = []

        p0 = contextlib.ExitStack(); all_stacks.append(p0)
        s_wst = [p0.enter_context(nc.sbuf_tensor(f"s_wst{i}", [128, 1024], F32, side="right")) for i in range(2)]
        s_wkT = p0.enter_context(nc.sbuf_tensor("s_wkT", [128, 8, 1024], BF16, side="right"))
        s_wvT = p0.enter_context(nc.sbuf_tensor("s_wvT", [128, 8, 1024], BF16, side="right"))
        vh_es = contextlib.ExitStack(); all_stacks.append(vh_es)
        vhat = vh_es.enter_context(nc.sbuf_tensor("vhat", [128, NB * VROW], BF16))
        pb_es = contextlib.ExitStack(); all_stacks.append(pb_es)
        p_bc = pb_es.enter_context(nc.psum_tensor("p_bc", [128, 512], F32))

        v0, t0 = C["v"], C["t"]
        with nc.Block("p0") as blk:
            @blk.sync
            def _(sync):
                sync = _W(sync)
                sync.dma_start(out=ident[:], in_=i_ident).then_inc(SC, 16)
                sync.dma_start(out=masks[:], in_=i_masks.rearrange("j t k q -> k j t q")).then_inc(SC, 16)
                sync.dma_start(out=klnw[:], in_=i_klnw).then_inc(SC, 16)
                sync.dma_start(out=qlnw[:], in_=i_qlnw).then_inc(SC, 16)
                sync.dma_start(out=qlnb[:], in_=i_qlnb).then_inc(SC, 16)
                sync.dma_start(out=sclr[:, 0:1], in_=i_s0).then_inc(SC, 16)
                sync.dma_start(out=sclr[:, 1:2], in_=i_s1).then_inc(SC, 16)
                sync.dma_start(out=sclr[:, 2:3], in_=i_s2).then_inc(SC, 16)
                for k in range(16):
                    wsrc = i_wkT if k < 8 else i_wvT
                    ec = k % 8
                    if k >= 2:
                        sync.wait_ge(SW[k % 2], 16 * (k // 2))
                        sync.wait_ge(SV, v0 + k)
                    sync.dma_start(out=s_wst[k % 2][:], in_=wsrc[ec * 128:(ec + 1) * 128, :]).then_inc(SW[k % 2], 16)
                sync.wait_ge(SW[0], 16 * 8)
                sync.wait_ge(SW[1], 16 * 8)
                sync.sem_inc(BAR, 1)

            @blk.vector
            def _(vector):
                vector = _W(vector)
                vector.wait_ge(SC, 16 * 8)
                vector.memset(onesf[:], 1.0)
                vector.memset(eps[:], LN_EPS)
                vector.memset(tiny[:], 1e-35)
                vector.memset(half[:], 0.5)
                vector.memset(mhalf[:], -0.5)
                vector.memset(two_b[:], 2.0)
                vector.memset(mone_b[:], -1.0)
                vector.memset(vhat[:], 1.0)
                vector.tensor_tensor(row_a[:], qlnw[:], klnw[:], ALU.mult)
                vector.tensor_tensor(row_b[:], qlnb[:], klnw[:], ALU.mult)
                vfence(vector)
                vector.tensor_scalar_mul(row_a[:], row_a[:], 0.125)
                vector.tensor_scalar_mul(row_b[:], row_b[:], 0.125)
                vector.mark(SV, 1)
                for k in range(16):
                    vector.wait_ge(SW[k % 2], 16 * (k // 2 + 1))
                    dst = s_wkT if k < 8 else s_wvT
                    vector.tensor_copy(dst[:, k % 8, :], s_wst[k % 2][:])
                    vector.mark(SV, 1)
                vector.wait_ge(ST, t0 + 1)
                vector.tensor_copy(wrep8[:], p_bc[:, 0:HD])
                vector.tensor_copy(brep8[:], p_bc[:, 64:64 + HD])
                vector.tensor_scalar(sc2s0[:], p_bc[:, 128:129], 2.0, None, ALU.mult)
                vector.tensor_scalar(sc2s1[:], p_bc[:, 129:130], 2.0, None, ALU.mult)
                vector.tensor_scalar(sc2s2[:], p_bc[:, 130:131], 2.0, None, ALU.mult)
                vector.mark(SV, 1)
                vector.wait_ge(SV, v0 + 18)
                vector.sem_inc(BAR, 1)

            @blk.tensor
            def _(tensor):
                tensor = _W(tensor)
                tensor.wait_ge(SV, v0 + 1)
                tensor.matmul(p_bc[:, 0:HD], onesf[:], row_a[:], start=True, stop=False)
                tensor.matmul(p_bc[:, 64:64 + HD], onesf[:], row_b[:], start=False, stop=False)
                tensor.matmul(p_bc[:, 128:129], onesf[:], sclr[:, 0:1], start=False, stop=False)
                tensor.matmul(p_bc[:, 129:130], onesf[:], sclr[:, 1:2], start=False, stop=False)
                tensor.matmul(p_bc[:, 130:131], onesf[:], sclr[:, 2:3], start=False, stop=True).then_inc(ST, 1)

            @blk.scalar
            def _(scalar):
                scalar = _W(scalar)
                scalar.sem_inc(BAR, 1)

            @blk.gpsimd
            def _(gpsimd):
                gpsimd = _W(gpsimd)
                gpsimd.sem_inc(BAR, 1)

        C["v"] += 18
        C["t"] += 1
        C["bar"] += 4

        if upto <= 0:
            stub = contextlib.ExitStack()
            s_z = stub.enter_context(nc.sbuf_tensor("s_zstub", [128, 1024], F32, side="right"))
            with nc.Block("stub") as blk:
                @blk.gpsimd
                def _(gpsimd):
                    gpsimd = _W(gpsimd)
                    gpsimd.wait_ge(BAR, C["bar"])
                    gpsimd.memset(s_z[:], 0.0)
                    for j in range(NJ):
                        if j >= 2:
                            gpsimd.wait_ge(SO[j % 2], 16 * (j // 2))
                        gpsimd.dma_start(out=o_out[j * 128:(j + 1) * 128, :], in_=s_z[:]).then_inc(SO[j % 2], 16)
                    gpsimd.wait_ge(SO[0], 16 * 8)
                    gpsimd.wait_ge(SO[1], 16 * 8)
                @blk.sync
                def _(sync):
                    sync = _W(sync)
                    sync.sem_inc(BAR, 1)
                @blk.vector
                def _(vector):
                    vector = _W(vector)
                    vector.sem_inc(BAR, 1)
                @blk.scalar
                def _(scalar):
                    scalar = _W(scalar)
                    scalar.sem_inc(BAR, 1)
            stub.close()
            for _st in reversed(all_stacks):
                _st.close()
            return nc

        pb_es.close()

        p1 = contextlib.ExitStack(); all_stacks.append(p1)
        scr32 = p1.enter_context(nc.sbuf_tensor("scr32_a", [128, 1024], F32, side="right"))
        s_xt = [p1.enter_context(nc.sbuf_tensor(f"s_xt{i}", [128, 8, 128], F32, side="right")) for i in range(2)]
        s_xtb = [p1.enter_context(nc.sbuf_tensor(f"s_xtb{i}", [128, 8, 128], BF16, side="right")) for i in range(2)]
        s_khat = [p1.enter_context(nc.sbuf_tensor(f"s_khat{i}", [128, 1024], BF16, side="right")) for i in range(2)]
        s_ktd = [p1.enter_context(nc.sbuf_tensor(f"s_ktd{i}", [128, 1024], BF16, side="right")) for i in range(2)]
        pp1 = contextlib.ExitStack(); all_stacks.append(pp1)
        p_k = [pp1.enter_context(nc.psum_tensor(f"p_k{i}", [128, 1024], F32)) for i in range(2)]
        p_v = [pp1.enter_context(nc.psum_tensor(f"p_v{i}", [128, 1024], F32)) for i in range(2)]

        t1, v1, s1_, b1 = C["t"], C["v"], C["s"], C["bar"]
        V1cast = lambda tb: v1 + tb + 1 if tb <= 1 else v1 + 3 * tb - 1
        V1var = lambda tb: v1 + 3 + 3 * tb
        V1app = lambda tb: v1 + 4 + 3 * tb
        T1kv = lambda tb: t1 + (1 if tb == 0 else 2 * tb)
        T1tr = lambda tb: t1 + (2 * tb + 3 if tb <= NB - 2 else 2 * NB)
        S1 = lambda tb, k: s1_ + 4 * tb + k

        with nc.Block("p1") as blk:
            @blk.sync
            def _(sync):
                sync = _W(sync)
                sync.wait_ge(BAR, b1)
                for tb in range(NB):
                    if tb >= 2:
                        sync.wait_ge(SXA[tb % 2], 16 * (tb // 2))
                        sync.wait_ge(SV, V1cast(tb - 2))
                    sync.dma_start(
                        out=s_xt[tb % 2][:],
                        in_=i_xT[:, tb * 128:(tb + 1) * 128].rearrange("(c p) t -> p c t", p=128),
                    ).then_inc(SXA[tb % 2], 16)
                sync.wait_ge(SXA[0], 16 * 16)
                sync.wait_ge(SXA[1], 16 * 16)
                sync.sem_inc(BAR, 1)

            @blk.tensor
            def _(tensor):
                tensor = _W(tensor)
                tensor.wait_ge(BAR, b1)

                def emit_kv(tb):
                    pp = tb % 2
                    tensor.wait_ge(SV, V1cast(tb))
                    if tb >= 2:
                        tensor.wait_ge(SV, V1app(tb - 2))
                        tensor.wait_ge(SS, S1(tb - 2, 4))
                    mm = None
                    for ec in range(8):
                        st, sp = (ec == 0), (ec == 7)
                        xb = s_xtb[pp][:, ec, :]
                        tensor.matmul(p_k[pp][:, 0:512], xb, s_wkT[:, ec, 0:512], start=st, stop=sp)
                        tensor.matmul(p_k[pp][:, 512:1024], xb, s_wkT[:, ec, 512:1024], start=st, stop=sp)
                        tensor.matmul(p_v[pp][:, 0:512], xb, s_wvT[:, ec, 0:512], start=st, stop=sp)
                        mm = tensor.matmul(p_v[pp][:, 512:1024], xb, s_wvT[:, ec, 512:1024], start=st, stop=sp)
                    mm.then_inc(ST, 1)

                def emit_tr(tb):
                    pp = tb % 2
                    tensor.wait_ge(SV, V1app(tb))
                    tensor.wait_ge(SS, S1(tb, 2))
                    mm = None
                    for c in range(8):
                        mm = tensor.matmul(p_v[pp][:, c * 128:(c + 1) * 128],
                                           s_khat[pp][:, c * 128:(c + 1) * 128], ident[:],
                                           start=(c % 4 == 0), stop=(c % 4 == 3))
                    mm.then_inc(ST, 1)

                emit_kv(0)
                for tb in range(1, NB):
                    emit_kv(tb)
                    emit_tr(tb - 1)
                emit_tr(NB - 1)

            @blk.scalar
            def _(scalar):
                scalar = _W(scalar)
                scalar.wait_ge(BAR, b1)
                for tb in range(NB):
                    pp = tb % 2
                    scalar.wait_ge(ST, T1kv(tb))
                    if tb >= 1:
                        scalar.wait_ge(SV, V1app(tb - 1))
                    scalar.activation(scr32[:], p_k[pp][:], AF.Square)
                    scalar.mark(SS, 1)
                    vh = vhat[:, tb * VROW:(tb + 1) * VROW].rearrange("p (g c) -> p g c", c=65)
                    scalar.copy(vh[:, 0:8, 0:HD], p_v[pp][:, 0:512].rearrange("p (g c) -> p g c", c=HD))
                    scalar.copy(vh[:, 8:16, 0:HD], p_v[pp][:, 512:1024].rearrange("p (g c) -> p g c", c=HD))
                    scalar.mark(SS, 1)
                    scalar.wait_ge(SV, V1var(tb))
                    scalar.activation(rk16[:], var16[:], AF.Ln, bias=eps[:])
                    scalar.activation(rk16[:], rk16[:], AF.Exp, scale=mhalf[:])
                    scalar.mark(SS, 1)
                    scalar.wait_ge(ST, T1tr(tb))
                    if tb >= 2:
                        scalar.wait_ge(SGA[pp], 16 * (tb // 2))
                    scalar.copy(s_ktd[pp][:, 0:512], p_v[pp][:, 0:512])
                    scalar.copy(s_ktd[pp][:, 512:1024], p_v[pp][:, 512:1024])
                    scalar.mark(SS, 1)
                scalar.wait_ge(SS, s1_ + 4 * NB)
                scalar.sem_inc(BAR, 1)

            @blk.vector
            def _(vector):
                vector = _W(vector)
                vector.wait_ge(BAR, b1)

                def emit_cast(tb):
                    pp = tb % 2
                    vector.wait_ge(SXA[pp], 16 * (tb // 2 + 1))
                    if tb >= 2:
                        vector.wait_ge(ST, T1kv(tb - 2))
                    vector.tensor_copy(s_xtb[pp][:], s_xt[pp][:])
                    vector.mark(SV, 1)

                emit_cast(0)
                emit_cast(1)
                for tb in range(NB):
                    pp = tb % 2
                    vector.wait_ge(ST, T1kv(tb))
                    vector.wait_ge(SS, S1(tb, 1))
                    vector.reduce_sum(mu16[:], p_k[pp][:].rearrange("p (g s) -> p g s", s=HD), axis=AX)
                    vector.reduce_sum(m216[:], scr32[:].rearrange("p (g s) -> p g s", s=HD), axis=AX)
                    vfence(vector)
                    vector.tensor_scalar_mul(mu16[:], mu16[:], 1.0 / HD)
                    vector.tensor_scalar_mul(m216[:], m216[:], 1.0 / HD)
                    vfence(vector)
                    vector.tensor_tensor(var16[:], mu16[:], mu16[:], ALU.mult)
                    vfence(vector)
                    vector.tensor_tensor(var16[:], m216[:], var16[:], ALU.subtract)
                    vector.mark(SV, 1)
                    vector.wait_ge(SS, S1(tb, 3))
                    mub = mu16[:].unsqueeze(2).broadcast_to([128, 16, HD])
                    rkb = rk16[:].unsqueeze(2).broadcast_to([128, 16, HD])
                    vector.tensor_tensor(scr32[:].rearrange("p (g s) -> p g s", s=HD),
                                         p_k[pp][:].rearrange("p (g s) -> p g s", s=HD), mub, ALU.subtract)
                    vfence(vector)
                    vector.tensor_tensor(s_khat[pp][:].rearrange("p (g s) -> p g s", s=HD),
                                         scr32[:].rearrange("p (g s) -> p g s", s=HD), rkb, ALU.mult)
                    vector.mark(SV, 1)
                    if tb + 2 < NB:
                        emit_cast(tb + 2)
                    else:
                        vector.memset(tiny[:], 1e-35)
                        vector.mark(SV, 1)
                vector.wait_ge(SV, v1 + 2 + 3 * NB)
                vector.sem_inc(BAR, 1)

            @blk.gpsimd
            def _(gpsimd):
                gpsimd = _W(gpsimd)
                gpsimd.wait_ge(BAR, b1)
                for tb in range(NB):
                    pp = tb % 2
                    gpsimd.wait_ge(SS, S1(tb, 4))
                    if tb >= 2:
                        gpsimd.wait_ge(SGA[pp], 16 * (tb // 2))
                    gpsimd.dma_start(
                        out=d_kT[:, :, tb * 128:(tb + 1) * 128].rearrange("c p t -> p c t"),
                        in_=s_ktd[pp][:].rearrange("p (c t) -> p c t", t=128),
                    ).then_inc(SGA[pp], 16)
                gpsimd.wait_ge(SGA[0], 16 * 16)
                gpsimd.wait_ge(SGA[1], 16 * 16)
                gpsimd.sem_inc(BAR, 1)

                C["t"] = t1 + 2 * NB
        C["v"] = v1 + 2 + 3 * NB
        C["s"] = s1_ + 4 * NB
        C["bar"] = b1 + 4

        if upto <= 1:
            stub = contextlib.ExitStack()
            s_z = stub.enter_context(nc.sbuf_tensor("s_zstub", [128, 1024], F32, side="right"))
            with nc.Block("stub") as blk:
                @blk.gpsimd
                def _(gpsimd):
                    gpsimd = _W(gpsimd)
                    gpsimd.wait_ge(BAR, C["bar"])
                    gpsimd.memset(s_z[:], 0.0)
                    for j in range(NJ):
                        if j >= 2:
                            gpsimd.wait_ge(SO[j % 2], 16 * (j // 2))
                        gpsimd.dma_start(out=o_out[j * 128:(j + 1) * 128, :], in_=s_z[:]).then_inc(SO[j % 2], 16)
                    gpsimd.wait_ge(SO[0], 16 * 8)
                    gpsimd.wait_ge(SO[1], 16 * 8)
                @blk.sync
                def _(sync):
                    sync = _W(sync)
                    sync.sem_inc(BAR, 1)
                @blk.vector
                def _(vector):
                    vector = _W(vector)
                    vector.sem_inc(BAR, 1)
                @blk.scalar
                def _(scalar):
                    scalar = _W(scalar)
                    scalar.sem_inc(BAR, 1)
            stub.close()
            for _st in reversed(all_stacks):
                _st.close()
            return nc

        pp1.close()

        p1b = contextlib.ExitStack(); all_stacks.append(p1b)
        s_wqT = p1b.enter_context(nc.sbuf_tensor("s_wqT", [128, 8, 1024], BF16, side="right"))
        s_t2q = p1b.enter_context(nc.sbuf_tensor("s_t2q", [128, 1024], F32, side="right"))
        s_qtl = [p1b.enter_context(nc.sbuf_tensor(f"s_qtl{i}", [128, 1024], BF16, side="right")) for i in range(2)]
        qT_es = contextlib.ExitStack(); all_stacks.append(qT_es)
        s_qT = qT_es.enter_context(nc.sbuf_tensor("s_qT", [128, 8, 2048], BF16))
        s_negB = qT_es.enter_context(nc.sbuf_tensor("s_negB", [16, 2048], BF16))
        ppb = contextlib.ExitStack(); all_stacks.append(ppb)
        p_q = [ppb.enter_context(nc.psum_tensor(f"p_q{i}", [128, 1024], F32)) for i in range(2)]
        p_qt = ppb.enter_context(nc.psum_tensor("p_qt", [128, 1024], F32))
        p_nb = ppb.enter_context(nc.psum_tensor("p_nb", [16, 128], F32))

        t2, v2, s2_, b2 = C["t"], C["v"], C["s"], C["bar"]
        V2cast = lambda j: v2 + 8 + j + 1 if j <= 1 else v2 + 10 + 6 * (j - 2) + 6
        V2var = lambda j: v2 + 10 + 6 * j + 1
        V2app = lambda j: v2 + 10 + 6 * j + 2
        V2nrm = lambda j: v2 + 10 + 6 * j + 3
        V2nb = lambda j: v2 + 10 + 6 * j + 4
        V2ev = lambda j: v2 + 10 + 6 * j + 5
        T2kv = lambda j: t2 + (1 if j == 0 else 2 * j)
        T2tr = lambda j: t2 + (2 * j + 3 if j <= NJ - 2 else 2 * NJ)
        S2 = lambda j, k: s2_ + 4 * j + k

        with nc.Block("p1b") as blk:
            @blk.sync
            def _(sync):
                sync = _W(sync)
                sync.wait_ge(BAR, b2)
                for k in range(8):
                    sync.wait_ge(SW[k % 2], 16 * (8 + k // 2))
                    if k >= 2:
                        sync.wait_ge(SV, v2 + k - 1)
                    sync.dma_start(out=s_wst[k % 2][:], in_=i_wqT[k * 128:(k + 1) * 128, :]).then_inc(SW[k % 2], 16)
                for j in range(NJ):
                    sync.wait_ge(SXA[j % 2], 16 * (16 + j // 2))
                    if j >= 2:
                        sync.wait_ge(SV, V2cast(j - 2))
                    sync.dma_start(
                        out=s_xt[j % 2][:],
                        in_=i_xqT[:, j * 128:(j + 1) * 128].rearrange("(c p) t -> p c t", p=128),
                    ).then_inc(SXA[j % 2], 16)
                sync.wait_ge(SXA[0], 16 * 24)
                sync.wait_ge(SXA[1], 16 * 24)
                sync.sem_inc(BAR, 1)

            @blk.vector
            def _(vector):
                vector = _W(vector)
                vector.wait_ge(BAR, b2)
                for k in range(8):
                    vector.wait_ge(SW[k % 2], 16 * (8 + k // 2 + 1))
                    vector.tensor_copy(s_wqT[:, k, :], s_wst[k % 2][:])
                    vector.mark(SV, 1)

                def emit_cast(j):
                    pp = j % 2
                    vector.wait_ge(SXA[pp], 16 * (16 + j // 2 + 1))
                    if j >= 2:
                        vector.wait_ge(ST, T2kv(j - 2))
                    vector.tensor_copy(s_xtb[pp][:], s_xt[pp][:])
                    vector.mark(SV, 1)

                emit_cast(0)
                emit_cast(1)
                for j in range(NJ):
                    pp = j % 2
                    vector.wait_ge(ST, T2kv(j))
                    vector.wait_ge(SS, S2(j, 1))
                    vector.reduce_sum(mu16[:], p_q[pp][:].rearrange("p (g s) -> p g s", s=HD), axis=AX)
                    vector.reduce_sum(m216[:], scr32[:].rearrange("p (g s) -> p g s", s=HD), axis=AX)
                    vfence(vector)
                    vector.tensor_scalar_mul(mu16[:], mu16[:], 1.0 / HD)
                    vector.tensor_scalar_mul(m216[:], m216[:], 1.0 / HD)
                    vfence(vector)
                    vector.tensor_tensor(var16[:], mu16[:], mu16[:], ALU.mult)
                    vfence(vector)
                    vector.tensor_tensor(var16[:], m216[:], var16[:], ALU.subtract)
                    vector.mark(SV, 1)
                    vector.wait_ge(SS, S2(j, 2))
                    mub = mu16[:].unsqueeze(2).broadcast_to([128, 16, HD])
                    rkb = rk16[:].unsqueeze(2).broadcast_to([128, 16, HD])
                    wrb = wrep8[:].unsqueeze(1).broadcast_to([128, 16, HD])
                    brb = brep8[:].unsqueeze(1).broadcast_to([128, 16, HD])
                    t2v = s_t2q[:].rearrange("p (g s) -> p g s", s=HD)
                    vector.tensor_tensor(t2v, p_q[pp][:].rearrange("p (g s) -> p g s", s=HD), mub, ALU.subtract)
                    vfence(vector)
                    vector.tensor_tensor(t2v, t2v, rkb, ALU.mult)
                    vfence(vector)
                    vector.tensor_tensor(t2v, t2v, wrb, ALU.mult)
                    vfence(vector)
                    vector.tensor_tensor(s_qtl[pp][:].rearrange("p (g s) -> p g s", s=HD), t2v, brb, ALU.add)
                    vector.mark(SV, 1)
                    vector.wait_ge(SS, S2(j, 3))
                    vector.reduce_sum(nrm16[:], scr32[:].rearrange("p (g s) -> p g s", s=HD), axis=AX)
                    vector.mark(SV, 1)
                    vector.wait_ge(SS, S2(j, 4))
                    vector.tensor_scalar(nb128[:], nrm16[:], -8.0, None, ALU.mult)
                    vector.mark(SV, 1)
                    vector.wait_ge(ST, T2tr(j))
                    vector.tensor_copy(s_negB[:, j * 128:(j + 1) * 128], p_nb[:])
                    vector.tensor_copy(s_qT[:, 0:4, j * 128:(j + 1) * 128],
                                       p_qt[:, 0:512].rearrange("p (c t) -> p c t", t=128))
                    vector.tensor_copy(s_qT[:, 4:8, j * 128:(j + 1) * 128],
                                       p_qt[:, 512:1024].rearrange("p (c t) -> p c t", t=128))
                    vector.mark(SV, 1)
                    if j + 2 < NJ:
                        emit_cast(j + 2)
                    else:
                        vector.memset(eps[:], LN_EPS)
                        vector.mark(SV, 1)
                vector.wait_ge(SV, V2ev(NJ - 1) + 1)
                vector.sem_inc(BAR, 1)

            @blk.tensor
            def _(tensor):
                tensor = _W(tensor)
                tensor.wait_ge(BAR, b2)

                def emit_q(j):
                    pp = j % 2
                    tensor.wait_ge(SV, V2cast(j))
                    if j >= 2:
                        tensor.wait_ge(SV, V2ev(j - 2))
                    mm = None
                    for ec in range(8):
                        st, sp = (ec == 0), (ec == 7)
                        xb = s_xtb[pp][:, ec, :]
                        tensor.matmul(p_q[pp][:, 0:512], xb, s_wqT[:, ec, 0:512], start=st, stop=sp)
                        mm = tensor.matmul(p_q[pp][:, 512:1024], xb, s_wqT[:, ec, 512:1024], start=st, stop=sp)
                    mm.then_inc(ST, 1)

                def emit_qtr(j):
                    pp = j % 2
                    tensor.wait_ge(SV, V2nb(j))
                    tensor.matmul(p_nb[:], nb128[:], ident[:], start=True, stop=True)
                    mm = None
                    for c in range(8):
                        mm = tensor.matmul(p_qt[:, c * 128:(c + 1) * 128],
                                           s_qtl[pp][:, c * 128:(c + 1) * 128], ident[:],
                                           start=(c % 4 == 0), stop=(c % 4 == 3))
                    mm.then_inc(ST, 1)

                emit_q(0)
                for j in range(1, NJ):
                    emit_q(j)
                    emit_qtr(j - 1)
                emit_qtr(NJ - 1)

            @blk.scalar
            def _(scalar):
                scalar = _W(scalar)
                scalar.wait_ge(BAR, b2)
                for j in range(NJ):
                    pp = j % 2
                    scalar.wait_ge(ST, T2kv(j))
                    if j >= 1:
                        scalar.wait_ge(SV, V2nrm(j - 1))
                    scalar.activation(scr32[:], p_q[pp][:], AF.Square)
                    scalar.mark(SS, 1)
                    scalar.wait_ge(SV, V2var(j))
                    scalar.activation(rk16[:], var16[:], AF.Ln, bias=eps[:])
                    scalar.activation(rk16[:], rk16[:], AF.Exp, scale=mhalf[:])
                    scalar.mark(SS, 1)
                    scalar.wait_ge(SV, V2app(j))
                    scalar.activation(scr32[:], s_qtl[pp][:], AF.Square)
                    scalar.mark(SS, 1)
                    scalar.wait_ge(SV, V2nrm(j))
                    scalar.activation(nrm16[:], nrm16[:], AF.Ln, bias=tiny[:])
                    scalar.activation(nrm16[:], nrm16[:], AF.Exp, scale=half[:])
                    scalar.mark(SS, 1)
                scalar.wait_ge(SS, S2(NJ - 1, 4))
                scalar.sem_inc(BAR, 1)

            @blk.gpsimd
            def _(gpsimd):
                gpsimd = _W(gpsimd)
                gpsimd.wait_ge(BAR, b2)
                gpsimd.sem_inc(BAR, 1)

        C["t"] = t2 + 2 * NJ
        C["v"] = v2 + 10 + 6 * NJ
        C["s"] = s2_ + 4 * NJ
        C["bar"] = b2 + 4

        if upto <= 2:
            stub = contextlib.ExitStack()
            s_z = stub.enter_context(nc.sbuf_tensor("s_zstub", [128, 1024], F32, side="right"))
            with nc.Block("stub") as blk:
                @blk.gpsimd
                def _(gpsimd):
                    gpsimd = _W(gpsimd)
                    gpsimd.wait_ge(BAR, C["bar"])
                    gpsimd.memset(s_z[:], 0.0)
                    for j in range(NJ):
                        if j >= 2:
                            gpsimd.wait_ge(SO[j % 2], 16 * (j // 2))
                        gpsimd.dma_start(out=o_out[j * 128:(j + 1) * 128, :], in_=s_z[:]).then_inc(SO[j % 2], 16)
                    gpsimd.wait_ge(SO[0], 16 * 8)
                    gpsimd.wait_ge(SO[1], 16 * 8)
                @blk.sync
                def _(sync):
                    sync = _W(sync)
                    sync.sem_inc(BAR, 1)
                @blk.vector
                def _(vector):
                    vector = _W(vector)
                    vector.sem_inc(BAR, 1)
                @blk.scalar
                def _(scalar):
                    scalar = _W(scalar)
                    scalar.sem_inc(BAR, 1)
            stub.close()
            for _st in reversed(all_stacks):
                _st.close()
            return nc

        ppb.close()
        p1b.close()
        p1.close()
        p0.close()

        ao_es = contextlib.ExitStack(); all_stacks.append(ao_es)
        s_aoT = ao_es.enter_context(nc.sbuf_tensor("s_aoT", [128, 8, 2048], BF16, side="right"))
        p2 = contextlib.ExitStack(); all_stacks.append(p2)
        s_khb = [p2.enter_context(nc.sbuf_tensor(f"s_khb{i}", [128, T], BF16, side="right")) for i in range(2)]
        s_qhb = [p2.enter_context(nc.sbuf_tensor(f"s_qhb{i}", [128, 2048], BF16, side="right")) for i in range(2)]
        s_at = [p2.enter_context(nc.sbuf_tensor(f"s_at{i}", [128, 1024], BF16, side="right")) for i in range(4)]
        s_rrep = p2.enter_context(nc.sbuf_tensor("s_rrep", [64, 2048], F32, side="right"))
        s_tmpO = p2.enter_context(nc.sbuf_tensor("s_tmpO", [64, 2048], BF16, side="right"))
        s_l = p2.enter_context(nc.sbuf_tensor("s_l", [1, 2048], F32, side="right"))
        s_r = p2.enter_context(nc.sbuf_tensor("s_r", [1, 2048], F32, side="right"))
        pp2 = contextlib.ExitStack(); all_stacks.append(pp2)
        p_o = pp2.enter_context(nc.psum_tensor("p_o", [128, 2048], F32))
        p_s = [pp2.enter_context(nc.psum_tensor(f"p_s{i}", [128, 1024], F32)) for i in range(2)]

        t3, av3, v3_, s3_, b3 = C["t"], C["av"], C["v"], C["s"], C["bar"]
        G2 = lambda h, k: h * NST + k
        VL = lambda h: v3_ + 1 + 2 * h + 1
        VN = lambda h: v3_ + 1 + 2 * h + 2

        with nc.Block("p2") as blk:
            @blk.sync
            def _(sync):
                sync = _W(sync)
                sync.wait_ge(BAR, b3)
                for h in range(H):
                    sync.wait_ge(SXA[h % 2], 16 * (24 + h // 2))
                    if h >= 2:
                        sync.wait_ge(ST, t3 + G2(h - 1, 0))
                    sync.dma_start(out=s_khb[h % 2][0:64, :],
                                   in_=d_kT[h // 2, (h % 2) * 64:(h % 2) * 64 + 64, :]).then_inc(SXA[h % 2], 16)
                sync.sem_inc(BAR, 1)

            @blk.gpsimd
            def _(gpsimd):
                gpsimd = _W(gpsimd)
                gpsimd.wait_ge(BAR, b3)
                for h in range(H):
                    if h >= 2:
                        gpsimd.wait_ge(SGB[h % 2], 32 * (h // 2))
                        gpsimd.wait_ge(ST, t3 + G2(h - 1, 0))
                    gpsimd.dma_start(out=s_qhb[h % 2][0:64, :],
                                     in_=s_qT[(h % 2) * 64:(h % 2) * 64 + 64, h // 2, :]).then_inc(SGB[h % 2], 16)
                    gpsimd.dma_start(out=s_qhb[h % 2][64:65, :],
                                     in_=s_negB[h:h + 1, :]).then_inc(SGB[h % 2], 16)
                    gpsimd.wait_ge(SV, VL(h))
                    if h >= 1:
                        gpsimd.wait_ge(SRB[(h - 1) % 2], 32 * ((h - 1) // 2 + 1))
                    if h >= 2:
                        gpsimd.wait_ge(SRB[h % 2], 32 * (h // 2))
                        gpsimd.wait_ge(SV, VN(h - 2))
                    gpsimd.dma_start(out=d_r, in_=s_r[:]).then_inc(SRB[h % 2], 16)
                    gpsimd.wait_ge(SRB[h % 2], 32 * (h // 2) + 16)
                    gpsimd.dma_start(out=s_rrep[:], in_=d_r.broadcast_to([64, TO])).then_inc(SRB[h % 2], 16)
                    if h % 2 == 1:
                        o = (h - 1) // 2
                        if o >= 2:
                            gpsimd.wait_ge(SGC[o % 2], 16 * (o // 2))
                        gpsimd.wait_ge(SV, VN(h))
                        gpsimd.dma_start(out=s_aoT[64:128, h // 2, :], in_=s_tmpO[:]).then_inc(SGC[o % 2], 16)
                gpsimd.wait_ge(SGC[0], 16 * 4)
                gpsimd.wait_ge(SGC[1], 16 * 4)
                gpsimd.sem_inc(BAR, 1)

            @blk.vector
            def _(vector):
                vector = _W(vector)
                vector.wait_ge(BAR, b3)
                vector.memset(s_khb[0][64:65, :], 1.0)
                vector.memset(s_khb[1][64:65, :], 1.0)
                vector.mark(SV, 1)
                for h in range(H):
                    for k in MASKED:
                        kb, c, r0, r1, mask = STEPS[k]
                        g = G2(h, k)
                        jm, tsel = mask
                        off = jm * 128 - r0
                        vector.wait_ge(SS, s3_ + g + 1)
                        vector.tensor_tensor(s_at[g % 4][:, off:off + 128],
                                             s_at[g % 4][:, off:off + 128],
                                             masks[:, jm, tsel, :], ALU.mult)
                        vector.mark(SM, 1)
                    vector.wait_ge(SAV, av3 + G2(h, NST))
                    vector.tensor_scalar(s_l[:, 0:1024], p_o[64:65, 0:1024], 1e-35, None, ALU.max)
                    vector.tensor_scalar(s_l[:, 1024:2048], p_o[64:65, 1024:2048], 1e-35, None, ALU.max)
                    vfence(vector)
                    vector.reciprocal(s_r[:], s_l[:])
                    vector.mark(SV, 1)
                    vector.wait_ge(SRB[h % 2], 32 * (h // 2 + 1))
                    if h % 2 == 0:
                        dst = s_aoT[0:64, h // 2, :]
                    else:
                        o = (h - 1) // 2
                        if o >= 1:
                            vector.wait_ge(SGC[(o - 1) % 2], 16 * ((o - 1) // 2 + 1))
                        dst = s_tmpO[:]
                    vector.tensor_tensor(dst, p_o[0:64, :], s_rrep[:], ALU.mult)
                    vector.mark(SV, 1)
                vector.wait_ge(SV, VN(H - 1))
                vector.sem_inc(BAR, 1)

            @blk.tensor
            def _(tensor):
                tensor = _W(tensor)
                tensor.wait_ge(BAR, b3)
                tensor.wait_ge(SV, v3_ + 1)

                def emit_s(h, k):
                    kb, c, r0, r1, mask = STEPS[k]
                    g = G2(h, k)
                    pp = g % 2
                    if g >= 2:
                        tensor.wait_ge(SS, s3_ + g - 1)
                    lo = c * 1024
                    mm = None
                    for (a, b_) in split512(r0 - lo, r1 - lo):
                        mm = tensor.matmul(p_s[pp][:, a:b_],
                                           s_khb[h % 2][0:65, kb * 128:(kb + 1) * 128],
                                           s_qhb[h % 2][0:65, lo + a:lo + b_],
                                           start=True, stop=True)
                    mm.then_inc(ST, 1)

                def emit_av(h, k):
                    kb, c, r0, r1, mask = STEPS[k]
                    g = G2(h, k)
                    if mask is not None:
                        tensor.wait_ge(SM, h * NMH + MPREF[k])
                    else:
                        tensor.wait_ge(SS, s3_ + g + 1)
                    if kb == 0 and h >= 1:
                        tensor.wait_ge(SV, VN(h - 1))
                    mm = None
                    for (a, b_) in split512(r0, r1):
                        bank = a // 512
                        mm = tensor.matmul(p_o[0:65, a:b_],
                                           vhat[:, kb * VROW + h * 65: kb * VROW + h * 65 + 65],
                                           s_at[g % 4][:, a - r0:b_ - r0],
                                           start=(kb == 0), stop=(kb == 8 * bank + 7))
                    mm.then_inc(SAV, 1)

                for h in range(H):
                    tensor.wait_ge(SXA[h % 2], 16 * (24 + h // 2 + 1))
                    tensor.wait_ge(SGB[h % 2], 32 * (h // 2 + 1))
                    for k in range(NST):
                        emit_s(h, k)
                        if k >= 3:
                            emit_av(h, k - 3)
                    emit_av(h, NST - 3)
                    emit_av(h, NST - 2)
                    emit_av(h, NST - 1)

            @blk.scalar
            def _(scalar):
                scalar = _W(scalar)
                scalar.wait_ge(BAR, b3)
                for h in range(H):
                    for k, (kb, c, r0, r1, mask) in enumerate(STEPS):
                        g = G2(h, k)
                        pp = g % 2
                        scalar.wait_ge(ST, t3 + g + 1)
                        if g >= 4:
                            scalar.wait_ge(SAV, av3 + g - 3)
                        scalar.activation(s_at[g % 4][:, 0:r1 - r0],
                                          p_s[pp][:, r0 - c * 1024:r1 - c * 1024], AF.Exp)
                        scalar.mark(SS, 1)
                scalar.wait_ge(SS, s3_ + H * NST)
                scalar.sem_inc(BAR, 1)

        C["t"] = t3 + H * NST
        C["av"] = av3 + H * NST
        C["v"] = v3_ + 1 + 2 * H
        C["s"] = s3_ + H * NST
        C["bar"] = b3 + 4

        if upto <= 3:
            stub = contextlib.ExitStack()
            s_z = stub.enter_context(nc.sbuf_tensor("s_zstub", [128, 1024], F32, side="right"))
            with nc.Block("stub") as blk:
                @blk.gpsimd
                def _(gpsimd):
                    gpsimd = _W(gpsimd)
                    gpsimd.wait_ge(BAR, C["bar"])
                    gpsimd.memset(s_z[:], 0.0)
                    for j in range(NJ):
                        if j >= 2:
                            gpsimd.wait_ge(SO[j % 2], 16 * (j // 2))
                        gpsimd.dma_start(out=o_out[j * 128:(j + 1) * 128, :], in_=s_z[:]).then_inc(SO[j % 2], 16)
                    gpsimd.wait_ge(SO[0], 16 * 8)
                    gpsimd.wait_ge(SO[1], 16 * 8)
                @blk.sync
                def _(sync):
                    sync = _W(sync)
                    sync.sem_inc(BAR, 1)
                @blk.vector
                def _(vector):
                    vector = _W(vector)
                    vector.sem_inc(BAR, 1)
                @blk.scalar
                def _(scalar):
                    scalar = _W(scalar)
                    scalar.sem_inc(BAR, 1)
            stub.close()
            for _st in reversed(all_stacks):
                _st.close()
            return nc

        pp2.close()
        p2.close()
        qT_es.close()
        vh_es.close()

        p3 = contextlib.ExitStack(); all_stacks.append(p3)
        scr32 = p3.enter_context(nc.sbuf_tensor("scr32_b", [128, 1024], F32, side="right"))
        s_u = p3.enter_context(nc.sbuf_tensor("s_u_b", [128, 1024], F32, side="right"))
        s_wst = [p3.enter_context(nc.sbuf_tensor(f"s_wstb{i}", [128, 1024], F32, side="right")) for i in range(2)]
        s_wuT = p3.enter_context(nc.sbuf_tensor("s_wuT", [128, 8, 1024], BF16, side="right"))
        s_xqb = [p3.enter_context(nc.sbuf_tensor(f"s_xqb{i}", [128, 1024], F32, side="right")) for i in range(2)]
        s_y = [p3.enter_context(nc.sbuf_tensor(f"s_y{i}", [128, 1024], F32, side="right")) for i in range(2)]
        s_yb = [p3.enter_context(nc.sbuf_tensor(f"s_yb{i}", [128, 1024], BF16, side="right")) for i in range(2)]
        yT_es = contextlib.ExitStack(); all_stacks.append(yT_es)
        s_yT = yT_es.enter_context(nc.sbuf_tensor("s_yT", [128, 8, 2048], BF16))
        pp3 = contextlib.ExitStack(); all_stacks.append(pp3)
        p_z = [pp3.enter_context(nc.psum_tensor(f"p_z{i}", [128, 1024], F32)) for i in range(2)]
        p_yt = pp3.enter_context(nc.psum_tensor("p_yt", [128, 1024], F32))

        t4, v4, s4_, b4 = C["t"], C["v"], C["s"], C["bar"]
        V4 = lambda j, k: v4 + 8 + 4 * j + k
        T4kv = lambda j: t4 + (1 if j == 0 else 2 * j)
        T4tr = lambda j: t4 + (2 * j + 3 if j <= NJ - 2 else 2 * NJ)
        S4 = lambda j, k: s4_ + 2 * j + k

        with nc.Block("p3") as blk:
            @blk.sync
            def _(sync):
                sync = _W(sync)
                sync.wait_ge(BAR, b4)
                for k in range(8):
                    sync.wait_ge(SW[k % 2], 16 * (12 + k // 2))
                    if k >= 2:
                        sync.wait_ge(SV, v4 + k - 1)
                    sync.dma_start(out=s_wst[k % 2][:], in_=i_wuT[k * 128:(k + 1) * 128, :]).then_inc(SW[k % 2], 16)
                for j in range(NJ):
                    sync.wait_ge(SXA[j % 2], 16 * (32 + j // 2))
                    if j >= 2:
                        sync.wait_ge(SV, V4(j - 2, 1))
                    sync.dma_start(out=s_xqb[j % 2][:], in_=i_xq[j * 128:(j + 1) * 128, :]).then_inc(SXA[j % 2], 16)
                sync.wait_ge(SXA[0], 16 * 40)
                sync.wait_ge(SXA[1], 16 * 40)
                sync.sem_inc(BAR, 1)

            @blk.vector
            def _(vector):
                vector = _W(vector)
                vector.wait_ge(BAR, b4)
                for k in range(8):
                    vector.wait_ge(SW[k % 2], 16 * (12 + k // 2 + 1))
                    vector.tensor_copy(s_wuT[:, k, :], s_wst[k % 2][:])
                    vector.mark(SV, 1)
                for j in range(NJ):
                    pp = j % 2
                    vector.wait_ge(ST, T4kv(j))
                    vector.wait_ge(SXA[pp], 16 * (32 + j // 2 + 1))
                    vector.tensor_tensor(s_u[:], p_z[pp][:], s_xqb[pp][:], ALU.add)
                    vfence(vector)
                    vector.reduce_sum(mu1[:], s_u[:], axis=AX)
                    vfence(vector)
                    vector.tensor_scalar_mul(mu1[:], mu1[:], 1.0 / D)
                    vector.mark(SV, 1)
                    vector.wait_ge(SS, S4(j, 1))
                    vector.reduce_sum(m21[:], scr32[:], axis=AX)
                    vfence(vector)
                    vector.tensor_scalar_mul(m21[:], m21[:], 1.0 / D)
                    vfence(vector)
                    vector.tensor_tensor(var1[:], mu1[:], mu1[:], ALU.mult)
                    vfence(vector)
                    vector.tensor_tensor(var1[:], m21[:], var1[:], ALU.subtract)
                    vector.mark(SV, 1)
                    vector.wait_ge(SS, S4(j, 2))
                    if j >= 2:
                        vector.wait_ge(SGA[pp], 16 * (16 + j // 2))
                    vector.tensor_scalar(s_y[pp][:], s_u[:], mu1[:], rs1[:], ALU.subtract, ALU.mult)
                    vfence(vector)
                    vector.tensor_copy(s_yb[pp][:], s_y[pp][:])
                    vector.mark(SV, 1)
                    vector.wait_ge(ST, T4tr(j))
                    vector.tensor_copy(s_yT[:, 0:4, j * 128:(j + 1) * 128],
                                       p_yt[:, 0:512].rearrange("p (c t) -> p c t", t=128))
                    vector.tensor_copy(s_yT[:, 4:8, j * 128:(j + 1) * 128],
                                       p_yt[:, 512:1024].rearrange("p (c t) -> p c t", t=128))
                    vector.mark(SV, 1)
                vector.wait_ge(SV, V4(NJ - 1, 4))
                vector.sem_inc(BAR, 1)

            @blk.tensor
            def _(tensor):
                tensor = _W(tensor)
                tensor.wait_ge(BAR, b4)
                tensor.wait_ge(SV, v4 + 8)

                def emit_wu(j):
                    pp = j % 2
                    if j >= 2:
                        tensor.wait_ge(SV, V4(j - 2, 4))
                    mm = None
                    for ec in range(8):
                        st, sp = (ec == 0), (ec == 7)
                        ao = s_aoT[:, ec, j * 128:(j + 1) * 128]
                        tensor.matmul(p_z[pp][:, 0:512], ao, s_wuT[:, ec, 0:512], start=st, stop=sp)
                        mm = tensor.matmul(p_z[pp][:, 512:1024], ao, s_wuT[:, ec, 512:1024], start=st, stop=sp)
                    mm.then_inc(ST, 1)

                def emit_ytr(j):
                    pp = j % 2
                    tensor.wait_ge(SV, V4(j, 3))
                    mm = None
                    for c in range(8):
                        mm = tensor.matmul(p_yt[:, c * 128:(c + 1) * 128],
                                           s_yb[pp][:, c * 128:(c + 1) * 128], ident[:],
                                           start=(c % 4 == 0), stop=(c % 4 == 3))
                    mm.then_inc(ST, 1)

                emit_wu(0)
                for j in range(1, NJ):
                    emit_wu(j)
                    emit_ytr(j - 1)
                emit_ytr(NJ - 1)

            @blk.scalar
            def _(scalar):
                scalar = _W(scalar)
                scalar.wait_ge(BAR, b4)
                for j in range(NJ):
                    scalar.wait_ge(SV, V4(j, 1))
                    scalar.activation(scr32[:], s_u[:], AF.Square)
                    scalar.mark(SS, 1)
                    scalar.wait_ge(SV, V4(j, 2))
                    scalar.activation(rs1[:], var1[:], AF.Ln, bias=eps[:])
                    scalar.activation(rs1[:], rs1[:], AF.Exp, scale=mhalf[:])
                    scalar.mark(SS, 1)
                scalar.wait_ge(SS, S4(NJ - 1, 2))
                scalar.sem_inc(BAR, 1)

            @blk.gpsimd
            def _(gpsimd):
                gpsimd = _W(gpsimd)
                gpsimd.wait_ge(BAR, b4)
                for j in range(NJ):
                    gpsimd.wait_ge(SV, V4(j, 3))
                    if j >= 2:
                        gpsimd.wait_ge(SGA[j % 2], 16 * (16 + j // 2))
                    gpsimd.dma_start(out=d_y[j], in_=s_y[j % 2][:]).then_inc(SGA[j % 2], 16)
                gpsimd.wait_ge(SGA[0], 16 * 24)
                gpsimd.wait_ge(SGA[1], 16 * 24)
                gpsimd.sem_inc(BAR, 1)

        C["t"] = t4 + 2 * NJ
        C["v"] = v4 + 8 + 4 * NJ
        C["s"] = s4_ + 2 * NJ
        C["bar"] = b4 + 4

        if upto <= 4:
            stub = contextlib.ExitStack()
            s_z = stub.enter_context(nc.sbuf_tensor("s_zstub", [128, 1024], F32, side="right"))
            with nc.Block("stub") as blk:
                @blk.gpsimd
                def _(gpsimd):
                    gpsimd = _W(gpsimd)
                    gpsimd.wait_ge(BAR, C["bar"])
                    gpsimd.memset(s_z[:], 0.0)
                    for j in range(NJ):
                        if j >= 2:
                            gpsimd.wait_ge(SO[j % 2], 16 * (j // 2))
                        gpsimd.dma_start(out=o_out[j * 128:(j + 1) * 128, :], in_=s_z[:]).then_inc(SO[j % 2], 16)
                    gpsimd.wait_ge(SO[0], 16 * 8)
                    gpsimd.wait_ge(SO[1], 16 * 8)
                @blk.sync
                def _(sync):
                    sync = _W(sync)
                    sync.sem_inc(BAR, 1)
                @blk.vector
                def _(vector):
                    vector = _W(vector)
                    vector.sem_inc(BAR, 1)
                @blk.scalar
                def _(scalar):
                    scalar = _W(scalar)
                    scalar.sem_inc(BAR, 1)
            stub.close()
            for _st in reversed(all_stacks):
                _st.close()
            return nc

        pp3.close()
        p3.close()
        ao_es.close()

        a1_es = contextlib.ExitStack(); all_stacks.append(a1_es)
        s_a1 = a1_es.enter_context(nc.sbuf_tensor("s_a1", [128, 32, 2048], BF16, side="right"))
        p4 = contextlib.ExitStack(); all_stacks.append(p4)
        s_k0 = [p4.enter_context(nc.sbuf_tensor(f"s_k0_{i}", [128, 16, 128], U8, side="right")) for i in range(2)]
        s_c0 = [p4.enter_context(nc.sbuf_tensor(f"s_c0_{i}", [128, 8, 128], BF16, side="right")) for i in range(2)]
        pp4 = contextlib.ExitStack(); all_stacks.append(pp4)
        p_h1 = [pp4.enter_context(nc.psum_tensor(f"p_h1_{i}", [128, 2048], F32)) for i in range(2)]

        t5, v5, s5_, b5 = C["t"], C["v"], C["s"], C["bar"]
        with nc.Block("p4") as blk:
            @blk.sync
            def _(sync):
                sync = _W(sync)
                sync.wait_ge(BAR, b5)
                for fc in range(32):
                    sync.wait_ge(SXA[fc % 2], 16 * (40 + fc // 2))
                    if fc >= 2:
                        sync.wait_ge(SV, v5 + fc - 1)
                    sync.dma_start(
                        out=s_k0[fc % 2][:],
                        in_=i_k0[:, fc * 128:(fc + 1) * 128].rearrange("(c p) f -> p c f", p=128),
                    ).then_inc(SXA[fc % 2], 16)
                sync.wait_ge(SXA[0], 16 * 56)
                sync.wait_ge(SXA[1], 16 * 56)
                sync.sem_inc(BAR, 1)

            @blk.vector
            def _(vector):
                vector = _W(vector)
                vector.wait_ge(BAR, b5)
                for fc in range(32):
                    vector.wait_ge(SXA[fc % 2], 16 * (40 + fc // 2 + 1))
                    if fc >= 2:
                        vector.wait_ge(ST, t5 + fc - 1)
                    vector.tensor_tensor(s_c0[fc % 2][:], s_k0[fc % 2][:, 0:8, :],
                                         s_k0[fc % 2][:, 8:16, :], ALU.subtract)
                    vector.mark(SV, 1)
                vector.wait_ge(SV, v5 + 32)
                vector.sem_inc(BAR, 1)

            @blk.tensor
            def _(tensor):
                tensor = _W(tensor)
                tensor.wait_ge(BAR, b5)
                for fc in range(32):
                    pp = fc % 2
                    tensor.wait_ge(SV, v5 + fc + 1)
                    if fc >= 2:
                        tensor.wait_ge(SS, s5_ + fc - 1)
                    mm = None
                    for ec in range(8):
                        st, sp = (ec == 0), (ec == 7)
                        w = s_c0[pp][:, ec, :]
                        for q in range(4):
                            mm = tensor.matmul(p_h1[pp][:, q * 512:(q + 1) * 512], w,
                                               s_yT[:, ec, q * 512:(q + 1) * 512], start=st, stop=sp)
                    mm.then_inc(ST, 1)

            @blk.scalar
            def _(scalar):
                scalar = _W(scalar)
                scalar.wait_ge(BAR, b5)
                for fc in range(32):
                    pp = fc % 2
                    scalar.wait_ge(ST, t5 + fc + 1)
                    scalar.activation(s_a1[:, fc, 0:1024], p_h1[pp][:, 0:1024], AF.Relu, scale=sc2s0[:])
                    scalar.activation(s_a1[:, fc, 1024:2048], p_h1[pp][:, 1024:2048], AF.Relu, scale=sc2s0[:])
                    scalar.mark(SS, 1)
                scalar.wait_ge(SS, s5_ + 32)
                scalar.sem_inc(BAR, 1)

            @blk.gpsimd
            def _(gpsimd):
                gpsimd = _W(gpsimd)
                gpsimd.wait_ge(BAR, b5)
                gpsimd.sem_inc(BAR, 1)

        C["t"] = t5 + 32
        C["v"] = v5 + 32
        C["s"] = s5_ + 32
        C["bar"] = b5 + 4

        if upto <= 5:
            stub = contextlib.ExitStack()
            s_z = stub.enter_context(nc.sbuf_tensor("s_zstub", [128, 1024], F32, side="right"))
            with nc.Block("stub") as blk:
                @blk.gpsimd
                def _(gpsimd):
                    gpsimd = _W(gpsimd)
                    gpsimd.wait_ge(BAR, C["bar"])
                    gpsimd.memset(s_z[:], 0.0)
                    for j in range(NJ):
                        if j >= 2:
                            gpsimd.wait_ge(SO[j % 2], 16 * (j // 2))
                        gpsimd.dma_start(out=o_out[j * 128:(j + 1) * 128, :], in_=s_z[:]).then_inc(SO[j % 2], 16)
                    gpsimd.wait_ge(SO[0], 16 * 8)
                    gpsimd.wait_ge(SO[1], 16 * 8)
                @blk.sync
                def _(sync):
                    sync = _W(sync)
                    sync.sem_inc(BAR, 1)
                @blk.vector
                def _(vector):
                    vector = _W(vector)
                    vector.sem_inc(BAR, 1)
                @blk.scalar
                def _(scalar):
                    scalar = _W(scalar)
                    scalar.sem_inc(BAR, 1)
            stub.close()
            for _st in reversed(all_stacks):
                _st.close()
            return nc

        pp4.close()
        p4.close()
        yT_es.close()

        p5 = contextlib.ExitStack(); all_stacks.append(p5)
        s_k1 = [p5.enter_context(nc.sbuf_tensor(f"s_k1_{i}", [128, 32, 128], U8, side="right")) for i in range(2)]
        s_w1 = [p5.enter_context(nc.sbuf_tensor(f"s_w1_{i}", [128, 32, 128], BF16, side="right")) for i in range(2)]
        s_a2t = [p5.enter_context(nc.sbuf_tensor(f"s_a2t{i}", [128, 2048], BF16, side="right")) for i in range(2)]
        pp5 = contextlib.ExitStack(); all_stacks.append(pp5)
        p_h2 = [pp5.enter_context(nc.psum_tensor(f"p_h2_{i}", [128, 2048], F32)) for i in range(2)]

        t6, v6, s6_, b6 = C["t"], C["v"], C["s"], C["bar"]
        S6c = lambda fc: s6_ + (1 if fc == 0 else 2 * fc)
        S6r = lambda fc: s6_ + (2 * fc + 3 if fc <= 29 else (63 if fc == 30 else 64))
        with nc.Block("p5") as blk:
            @blk.sync
            def _(sync):
                sync = _W(sync)
                sync.wait_ge(BAR, b6)
                for fc in range(32):
                    sync.wait_ge(SXA[fc % 2], 16 * (56 + fc // 2))
                    if fc >= 2:
                        sync.wait_ge(SV, v6 + fc - 1)
                        sync.wait_ge(SS, S6c(fc - 2))
                    sync.dma_start(
                        out=s_k1[fc % 2][:],
                        in_=i_k1[:, fc * 128:(fc + 1) * 128].rearrange("(c p) f -> p c f", p=128),
                    ).then_inc(SXA[fc % 2], 16)
                sync.wait_ge(SXA[0], 16 * 72)
                sync.wait_ge(SXA[1], 16 * 72)
                sync.sem_inc(BAR, 1)

            @blk.vector
            def _(vector):
                vector = _W(vector)
                vector.wait_ge(BAR, b6)
                for fc in range(32):
                    vector.wait_ge(SXA[fc % 2], 16 * (56 + fc // 2 + 1))
                    if fc >= 2:
                        vector.wait_ge(ST, t6 + fc - 1)
                    vector.tensor_scalar(s_w1[fc % 2][:, 0:16, :], s_k1[fc % 2][:, 0:16, :],
                                         2.0, -1.0, ALU.mult, ALU.add)
                    vector.mark(SV, 1)
                vector.wait_ge(SV, v6 + 32)
                vector.sem_inc(BAR, 1)

            @blk.scalar
            def _(scalar):
                scalar = _W(scalar)
                scalar.wait_ge(BAR, b6)

                def emit_conv(fc):
                    pp = fc % 2
                    scalar.wait_ge(SXA[pp], 16 * (56 + fc // 2 + 1))
                    if fc >= 2:
                        scalar.wait_ge(ST, t6 + fc - 1)
                    scalar.sign(s_w1[pp][:, 16:32, :], s_k1[pp][:, 16:32, :], bias=mhalf[:])
                    scalar.mark(SS, 1)

                def emit_relu(fc):
                    pp = fc % 2
                    scalar.wait_ge(ST, t6 + fc + 1)
                    if fc >= 2:
                        scalar.wait_ge(SGA[pp], 16 * (24 + fc // 2))
                    scalar.activation(s_a2t[pp][:, 0:1024], p_h2[pp][:, 0:1024], AF.Relu, scale=sc2s1[:])
                    scalar.activation(s_a2t[pp][:, 1024:2048], p_h2[pp][:, 1024:2048], AF.Relu, scale=sc2s1[:])
                    scalar.mark(SS, 1)

                emit_conv(0)
                emit_conv(1)
                for fc in range(30):
                    emit_relu(fc)
                    emit_conv(fc + 2)
                emit_relu(30)
                emit_relu(31)
                scalar.wait_ge(SS, s6_ + 64)
                scalar.sem_inc(BAR, 1)

            @blk.tensor
            def _(tensor):
                tensor = _W(tensor)
                tensor.wait_ge(BAR, b6)
                for fc in range(32):
                    pp = fc % 2
                    tensor.wait_ge(SV, v6 + fc + 1)
                    tensor.wait_ge(SS, S6c(fc))
                    if fc >= 2:
                        tensor.wait_ge(SS, S6r(fc - 2))
                    mm = None
                    for f0c in range(32):
                        st, sp = (f0c == 0), (f0c == 31)
                        w = s_w1[pp][:, f0c, :]
                        for q in range(4):
                            mm = tensor.matmul(p_h2[pp][:, q * 512:(q + 1) * 512], w,
                                               s_a1[:, f0c, q * 512:(q + 1) * 512], start=st, stop=sp)
                    mm.then_inc(ST, 1)

            @blk.gpsimd
            def _(gpsimd):
                gpsimd = _W(gpsimd)
                gpsimd.wait_ge(BAR, b6)
                for fc in range(32):
                    gpsimd.wait_ge(SS, S6r(fc))
                    if fc >= 2:
                        gpsimd.wait_ge(SGA[fc % 2], 16 * (24 + fc // 2))
                    gpsimd.dma_start(out=d_a2[fc], in_=s_a2t[fc % 2][:]).then_inc(SGA[fc % 2], 16)
                gpsimd.wait_ge(SGA[0], 16 * 40)
                gpsimd.wait_ge(SGA[1], 16 * 40)
                gpsimd.sem_inc(BAR, 1)

        C["t"] = t6 + 32
        C["v"] = v6 + 32
        C["s"] = s6_ + 64
        C["bar"] = b6 + 4

        if upto <= 6:
            stub = contextlib.ExitStack()
            s_z = stub.enter_context(nc.sbuf_tensor("s_zstub", [128, 1024], F32, side="right"))
            with nc.Block("stub") as blk:
                @blk.gpsimd
                def _(gpsimd):
                    gpsimd = _W(gpsimd)
                    gpsimd.wait_ge(BAR, C["bar"])
                    gpsimd.memset(s_z[:], 0.0)
                    for j in range(NJ):
                        if j >= 2:
                            gpsimd.wait_ge(SO[j % 2], 16 * (j // 2))
                        gpsimd.dma_start(out=o_out[j * 128:(j + 1) * 128, :], in_=s_z[:]).then_inc(SO[j % 2], 16)
                    gpsimd.wait_ge(SO[0], 16 * 8)
                    gpsimd.wait_ge(SO[1], 16 * 8)
                @blk.sync
                def _(sync):
                    sync = _W(sync)
                    sync.sem_inc(BAR, 1)
                @blk.vector
                def _(vector):
                    vector = _W(vector)
                    vector.sem_inc(BAR, 1)
                @blk.scalar
                def _(scalar):
                    scalar = _W(scalar)
                    scalar.sem_inc(BAR, 1)
            stub.close()
            for _st in reversed(all_stacks):
                _st.close()
            return nc

        pp5.close()
        p5.close()
        a1_es.close()

        p6 = contextlib.ExitStack(); all_stacks.append(p6)
        scr32 = p6.enter_context(nc.sbuf_tensor("scr32_c", [128, 1024], F32, side="right"))
        s_u = p6.enter_context(nc.sbuf_tensor("s_u_c", [128, 1024], F32, side="right"))
        s_k2s = [p6.enter_context(nc.sbuf_tensor(f"s_k2s{i}", [128, 1024], U8, side="right")) for i in range(2)]
        s_k2b = p6.enter_context(nc.sbuf_tensor("s_k2b", [128, 32, 1024], BF16, side="right"))
        s_a2b = [p6.enter_context(nc.sbuf_tensor(f"s_a2b{i}", [128, 32, 128], BF16, side="right")) for i in range(2)]
        s_yr = [p6.enter_context(nc.sbuf_tensor(f"s_yr{i}", [128, 1024], F32, side="right")) for i in range(2)]
        s_ou = [p6.enter_context(nc.sbuf_tensor(f"s_ou{i}", [128, 1024], F32, side="right")) for i in range(2)]
        pp6 = contextlib.ExitStack(); all_stacks.append(pp6)
        p_h3 = [pp6.enter_context(nc.psum_tensor(f"p_h3_{i}", [128, 1024], F32)) for i in range(2)]

        t7, v7, s7_, b7 = C["t"], C["v"], C["s"], C["bar"]
        V7 = lambda j, k: v7 + 32 + 3 * j + k
        T7 = lambda j: t7 + j + 1
        S7 = lambda j, k: s7_ + 2 * j + k
        with nc.Block("p6") as blk:
            @blk.sync
            def _(sync):
                sync = _W(sync)
                sync.wait_ge(BAR, b7)
                for i in range(32):
                    sync.wait_ge(SXA[i % 2], 16 * (72 + i // 2))
                    if i >= 2:
                        sync.wait_ge(SV, v7 + i - 1)
                    sync.dma_start(out=s_k2s[i % 2][:], in_=i_k2[i * 128:(i + 1) * 128, :]).then_inc(SXA[i % 2], 16)
                for j in range(NJ):
                    if j >= 2:
                        sync.wait_ge(SXB[j % 2], 16 * (j // 2))
                        sync.wait_ge(ST, T7(j - 2))
                    sync.dma_start(
                        out=s_a2b[j % 2][:],
                        in_=d_a2[:, :, j * 128:(j + 1) * 128].rearrange("c p t -> p c t"),
                    ).then_inc(SXB[j % 2], 16)
                    if j >= 2:
                        sync.wait_ge(SXC[j % 2], 16 * (j // 2))
                        sync.wait_ge(SV, V7(j - 2, 1))
                    sync.dma_start(out=s_yr[j % 2][:], in_=d_y[j]).then_inc(SXC[j % 2], 16)

            @blk.vector
            def _(vector):
                vector = _W(vector)
                vector.wait_ge(BAR, b7)
                for i in range(32):
                    vector.wait_ge(SXA[i % 2], 16 * (72 + i // 2 + 1))
                    vector.tensor_scalar(s_k2b[:, i, :], s_k2s[i % 2][:], 2.0, -1.0, ALU.mult, ALU.add)
                    vector.mark(SV, 1)
                for j in range(NJ):
                    pp = j % 2
                    vector.wait_ge(ST, T7(j))
                    vector.wait_ge(SXC[pp], 16 * (j // 2 + 1))
                    vector.tensor_scalar(s_u[:], p_h3[pp][:], sc2s2[:], None, ALU.mult)
                    vfence(vector)
                    vector.tensor_tensor(s_u[:], s_u[:], s_yr[pp][:], ALU.add)
                    vfence(vector)
                    vector.reduce_sum(mu1[:], s_u[:], axis=AX)
                    vfence(vector)
                    vector.tensor_scalar_mul(mu1[:], mu1[:], 1.0 / D)
                    vector.mark(SV, 1)
                    vector.wait_ge(SS, S7(j, 1))
                    vector.reduce_sum(m21[:], scr32[:], axis=AX)
                    vfence(vector)
                    vector.tensor_scalar_mul(m21[:], m21[:], 1.0 / D)
                    vfence(vector)
                    vector.tensor_tensor(var1[:], mu1[:], mu1[:], ALU.mult)
                    vfence(vector)
                    vector.tensor_tensor(var1[:], m21[:], var1[:], ALU.subtract)
                    vector.mark(SV, 1)
                    vector.wait_ge(SS, S7(j, 2))
                    if j >= 2:
                        vector.wait_ge(SO[pp], 16 * (j // 2))
                    vector.tensor_scalar(s_ou[pp][:], s_u[:], mu1[:], rs1[:], ALU.subtract, ALU.mult)
                    vector.mark(SV, 1)

            @blk.tensor
            def _(tensor):
                tensor = _W(tensor)
                tensor.wait_ge(BAR, b7)
                tensor.wait_ge(SV, v7 + 32)
                for j in range(NJ):
                    pp = j % 2
                    tensor.wait_ge(SXB[pp], 16 * (j // 2 + 1))
                    if j >= 2:
                        tensor.wait_ge(SV, V7(j - 2, 1))
                    mm = None
                    for f1c in range(32):
                        st, sp = (f1c == 0), (f1c == 31)
                        a2 = s_a2b[pp][:, f1c, :]
                        tensor.matmul(p_h3[pp][:, 0:512], a2, s_k2b[:, f1c, 0:512], start=st, stop=sp)
                        mm = tensor.matmul(p_h3[pp][:, 512:1024], a2, s_k2b[:, f1c, 512:1024], start=st, stop=sp)
                    mm.then_inc(ST, 1)

            @blk.scalar
            def _(scalar):
                scalar = _W(scalar)
                scalar.wait_ge(BAR, b7)
                for j in range(NJ):
                    scalar.wait_ge(SV, V7(j, 1))
                    scalar.activation(scr32[:], s_u[:], AF.Square)
                    scalar.mark(SS, 1)
                    scalar.wait_ge(SV, V7(j, 2))
                    scalar.activation(rs1[:], var1[:], AF.Ln, bias=eps[:])
                    scalar.activation(rs1[:], rs1[:], AF.Exp, scale=mhalf[:])
                    scalar.mark(SS, 1)

            @blk.gpsimd
            def _(gpsimd):
                gpsimd = _W(gpsimd)
                gpsimd.wait_ge(BAR, b7)
                for j in range(NJ):
                    gpsimd.wait_ge(SV, V7(j, 3))
                    if j >= 2:
                        gpsimd.wait_ge(SO[j % 2], 16 * (j // 2))
                    gpsimd.dma_start(out=o_out[j * 128:(j + 1) * 128, :], in_=s_ou[j % 2][:]).then_inc(SO[j % 2], 16)
                gpsimd.wait_ge(SO[0], 16 * 8)
                gpsimd.wait_ge(SO[1], 16 * 8)

        if K_DBG:
            with nc.Block("dbg") as blk:
                @blk.gpsimd
                def _(gpsimd):
                    gpsimd = _W(gpsimd)
                    gpsimd.wait_ge(SO[0], 16 * 8)
                    gpsimd.wait_ge(SO[1], 16 * 8)
                    gpsimd.dma_start(out=o_dbgs, in_=s_dbgs[:]).then_inc(SGC[0], 16)
                    gpsimd.dma_start(out=o_dbgy, in_=d_y[:]).then_inc(SGC[0], 16)
                    gpsimd.dma_start(out=o_dbgk, in_=d_kT[:]).then_inc(SGC[0], 16)
                    gpsimd.dma_start(out=o_dbga, in_=d_a2[0:4]).then_inc(SGC[0], 16)
                    gpsimd.wait_ge(SGC[0], 16 * 4 + 16 * 4)
        pp6.close()
        p6.close()

    return nc


_NC_CACHE = None


def kernel(L0_attn_x, L0_attn_Wk, L0_attn_Wq, L0_attn_Wv, L0_attn_Wu,
           L0_attn_kln_w, L0_attn_kln_b, L0_attn_qln_w, L0_attn_qln_b,
           L0_ffn_k0, L0_ffn_k1, L0_ffn_k2, L0_ffn_s0, L0_ffn_s1, L0_ffn_s2):
    global _NC_CACHE
    x = np.ascontiguousarray(np.asarray(L0_attn_x, dtype=np.float32))
    wkT = np.ascontiguousarray(np.asarray(L0_attn_Wk, np.float32).T)
    wqT = np.ascontiguousarray(np.asarray(L0_attn_Wq, np.float32).T)
    wvT = np.ascontiguousarray(np.asarray(L0_attn_Wv, np.float32).T)
    wuT = np.ascontiguousarray(np.asarray(L0_attn_Wu, np.float32).T)
    klnw = np.ascontiguousarray(np.asarray(L0_attn_kln_w, np.float32).reshape(1, HD))
    qlnw = np.ascontiguousarray(np.asarray(L0_attn_qln_w, np.float32).reshape(1, HD))
    qlnb = np.ascontiguousarray(np.asarray(L0_attn_qln_b, np.float32).reshape(1, HD))
    k0 = np.ascontiguousarray(np.asarray(L0_ffn_k0)).view(np.uint8)
    k1 = np.ascontiguousarray(np.asarray(L0_ffn_k1)).view(np.uint8)
    k2 = np.ascontiguousarray(np.asarray(L0_ffn_k2)).view(np.uint8)
    s0 = np.ascontiguousarray(np.asarray(L0_ffn_s0, np.float32).reshape(1, 1))
    s1 = np.ascontiguousarray(np.asarray(L0_ffn_s1, np.float32).reshape(1, 1))
    s2 = np.ascontiguousarray(np.asarray(L0_ffn_s2, np.float32).reshape(1, 1))
    ident = np.eye(128).astype(bf16)

    in_maps = []
    rows_per_core = []
    q_i = np.arange(128)[:, None]
    k_i = np.arange(128)[None, :]
    for c in range(8):
        b, role = c // 2, c % 2
        obs = own_blocks(role)
        rows = np.concatenate([np.arange(g * 128, (g + 1) * 128) for g in obs])
        rows_per_core.append((b, rows))
        xb = x[b]
        xq = np.ascontiguousarray(xb[rows])
        m = np.zeros((NJ, 2, 128, 128), np.float32)
        for j in range(NJ):
            g = obs[j]
            for t in range(2):
                kb = 2 * j + t
                if kb == g:
                    m[j, t] = (q_i >= k_i).astype(np.float32).T
                elif kb < g:
                    m[j, t] = 1.0
        in_maps.append(dict(
            xT=np.ascontiguousarray(xb.T), xq=xq, xqT=np.ascontiguousarray(xq.T),
            wkT=wkT, wvT=wvT, wqT=wqT, wuT=wuT,
            klnw=klnw, qlnw=qlnw, qlnb=qlnb,
            k0=k0, k1=k1, k2=k2, s0=s0, s1=s1, s2=s2,
            ident=ident, masks=np.ascontiguousarray(m.astype(bf16)),
        ))

    if _NC_CACHE is None:
        _NC_CACHE = build_program()
    res = run_bass_kernel_spmd(_NC_CACHE, in_maps, core_ids=list(range(8)))
    out = np.empty((B, T, D), np.float32)
    for c in range(8):
        b, rows = rows_per_core[c]
        out[b][rows] = res.results[c]["out"]
    return out
